# version 05
# baseline (speedup 1.0000x reference)
import math
import os
from contextlib import ExitStack

import numpy as np

import concourse.bass as bass
import concourse.mybir as mybir
from concourse.bass_utils import run_bass_kernel_spmd

F32 = mybir.dt.float32
BF16 = mybir.dt.bfloat16
I32 = mybir.dt.int32
AF = mybir.ActivationFunctionType
ALU = mybir.AluOpType
AX = mybir.AxisListType.X

NCORES = 8
T = 1024
TT = 512
D = 2048
DC = 16
II = 5504
IC = 43
H = 16
EPS = 1e-5
RS = 1.0 / math.sqrt(128.0)
SQD = math.sqrt(128.0)

STAGE = int(os.environ.get("BASS_MK_STAGE", "3"))

FFN_BLOCKS = [(c, min(512, II - c)) for c in range(0, II, 512)]


def build_nc():
    nc = bass.Bass("TRN2", target_bir_lowering=False, detect_race_conditions=False)

    dx = nc.dram_tensor("x", [T, D], F32, kind="ExternalInput").ap()
    dgidx = nc.dram_tensor("gidx", [II, 256], I32, kind="ExternalInput").ap()
    duidx = nc.dram_tensor("uidx", [II, 256], I32, kind="ExternalInput").ap()
    ddidx = nc.dram_tensor("didx", [D, 688], I32, kind="ExternalInput").ap()
    dlut = nc.dram_tensor("lut", [16], F32, kind="ExternalInput").ap()
    ddense = {nm: nc.dram_tensor(nm, [D, 256], F32, kind="ExternalInput").ap()
              for nm in ("wq", "wk", "wv", "wo")}
    dbias = {nm: nc.dram_tensor(nm, [D], F32, kind="ExternalInput").ap()
             for nm in ("bq", "bk", "bv", "bo")}
    dy = nc.dram_tensor("y", [T, D], F32, kind="ExternalOutput").ap()

    wg_part = nc.dram_tensor("wg_part", [256, II], BF16, kind="Internal").ap()
    wu_part = nc.dram_tensor("wu_part", [256, II], BF16, kind="Internal").ap()
    wd_part = nc.dram_tensor("wd_part", [688, D], BF16, kind="Internal").ap()
    den_part = nc.dram_tensor("den_part", [1024, D], BF16, kind="Internal").ap()
    kv_part = nc.dram_tensor("kv_part", [4096, T], BF16, kind="Internal").ap()
    wg_all = nc.dram_tensor("wg_all", [D, II], BF16, kind="Internal").ap()
    wu_all = nc.dram_tensor("wu_all", [D, II], BF16, kind="Internal").ap()
    wd_all = nc.dram_tensor("wd_all", [II, D], BF16, kind="Internal").ap()
    den_all = nc.dram_tensor("den_all", [8192, D], BF16, kind="Internal").ap()
    kv_all = nc.dram_tensor("kv_all", [16384, T], BF16, kind="Internal").ap()
    k_all = kv_all[0:8192, :]
    v_all = kv_all[8192:16384, :]

    es = ExitStack()

    def sb(name, shape, dtype):
        return es.enter_context(nc.sbuf_tensor(name, shape, dtype))

    xs = sb("xs", [128, DC * T], F32)
    xl = sb("xl", [128, DC * TT], BF16)
    RA = sb("RA", [128, 32768], BF16)
    RB = sb("RB", [128, 23552], BF16)
    Abc = sb("Abc", [128, 512], F32)
    Bbc = sb("Bbc", [128, 512], F32)
    tmpn = sb("tmpn", [128, 512], F32)
    rowp = sb("rowp", [128, 512], F32)
    rowp2 = sb("rowp2", [128, 512], F32)
    identb = sb("identb", [128, 128], BF16)
    identf = sb("identf", [128, 128], F32)
    onesb = sb("onesb", [128, 512], BF16)
    onesf = sb("onesf", [128, 128], F32)
    brow = sb("brow", [64, D], BF16)
    bcol = sb("bcol", [128, 32], F32)
    lutb = sb("lutb", [128, 16], F32)
    crow = sb("crow", [128, 16], F32)

    x_stage = [RA[:, 16384 + 4096 * i: 16384 + 4096 * (i + 1)].bitcast(F32) for i in range(2)]
    wave_i32 = [RA[:, 2048 * i: 2048 * (i + 1)].bitcast(I32) for i in range(2)]
    wave_bf = [RA[:, 4096 + 1024 * i: 4096 + 1024 * (i + 1)] for i in range(2)]
    idxT = [[RA[:, 6144 + 1024 * (2 * pp + hb): 6144 + 1024 * (2 * pp + hb + 1)]
             for hb in range(2)] for pp in range(2)]
    wsel = [RA[:, 10368 + 1024 * i: 10368 + 1024 * (i + 1)] for i in range(2)]
    term = RA[:, 12416: 13440]
    dstage_i32 = [RA[:, 1376 * i: 1376 * (i + 1)].bitcast(I32) for i in range(2)]
    dstage_bf = [RA[:, 2752 + 688 * i: 2752 + 688 * (i + 1)] for i in range(2)]
    dT = [RA[:, 4128 + 1024 * g: 4128 + 1024 * (g + 1)] for g in range(6)]
    den_stage = [RA[:, 512 * i: 512 * (i + 1)].bitcast(F32) for i in range(2)]
    den_bf = [RA[:, 1024 + 256 * i: 1024 + 256 * (i + 1)] for i in range(2)]
    denT = [RA[:, 4096 + 2048 * hb: 4096 + 2048 * (hb + 1)] for hb in range(2)]
    wbuf = [RA[:, 8192 * i: 8192 * (i + 1)] for i in range(2)]
    wbufq = [RA[:, 4096 * i: 4096 * (i + 1)] for i in range(2)]
    xl_t1 = RA[:, 8192: 16384]
    Qb = RA[:, 16384: 32768]
    wave2_i32 = [RA[:, 16384 + 2048 * i: 16384 + 2048 * (i + 1)].bitcast(I32) for i in range(2)]
    wave2_bf = [RA[:, 20480 + 1024 * i: 20480 + 1024 * (i + 1)] for i in range(2)]
    idxT2 = [[RA[:, 22528 + 1024 * (2 * pp + hb): 22528 + 1024 * (2 * pp + hb + 1)]
              for hb in range(2)] for pp in range(2)]
    wsel2 = [RA[:, 28672 + 1024 * i: 28672 + 1024 * (i + 1)] for i in range(2)]
    term2 = RA[:, 30720: 31744]
    dstage2_i32 = [RA[:, 16384 + 1536 * i: 16384 + 1536 * (i + 1)].bitcast(I32) for i in range(2)]
    dstage2_bf = [RA[:, 19456 + 1536 * i: 19456 + 1536 * (i + 1)] for i in range(2)]
    dT2 = [RA[:, 22528 + 1024 * g: 22528 + 1024 * (g + 1)] for g in range(6)]
    den2_stage = [RA[:, 16384 + 512 * i: 16384 + 512 * (i + 1)].bitcast(F32) for i in range(2)]
    den2_bf = [RA[:, 17408 + 512 * i: 17408 + 512 * (i + 1)] for i in range(2)]
    denT2 = [RA[:, 18432 + 2048 * hb: 18432 + 2048 * (hb + 1)] for hb in range(2)]
    Kh = [RA[:, 4096 * i: 4096 * (i + 1)] for i in range(2)]
    Vb = [RA[:, 8192 + 2048 * i: 8192 + 2048 * (i + 1)] for i in range(3)]
    ostage = [RA[:, 4096 * i: 4096 * (i + 1)].bitcast(F32) for i in range(2)]

    h_act = RB[:, 0: IC * 512]
    xsq = [RB[:, 22016 + 512 * i: 22016 + 512 * (i + 1)] for i in range(2)]
    attn_s = RB[:, 0: 16384]
    PTv = [RB[:, 16384 + 1024 * hh: 16384 + 1024 * (hh + 1)] for hh in range(2)]
    sacc4 = [RB[:, 18432 + 1024 * i: 18432 + 1024 * (i + 1)].bitcast(F32)
             for i in range(4)]
    sqs = [RB[:, 22528 + 512 * i: 22528 + 512 * (i + 1)] for i in range(2)]
    kstage = [RB[:, 512 * i: 512 * (i + 1)] for i in range(2)]
    vstage = [RB[:, 1024 + 512 * i: 1024 + 512 * (i + 1)] for i in range(2)]

    def psum(name, shape, dtype=F32):
        return es.enter_context(nc.psum_tensor(name, shape, dtype))

    P2 = [psum(f"P2_{i}", [128, 1024], F32) for i in range(4)]
    PB = [P2[0][:, 0:512], P2[0][:, 512:1024], P2[1][:, 0:512], P2[1][:, 512:1024],
          P2[2][:, 0:512], P2[2][:, 512:1024]]
    PR = P2[3][:, 0:512]

    sems = {}
    for s in ("sG", "sCC", "sPE", "sACT", "sDVE", "sPOOL",
              "sX0", "sX1", "sWB0", "sWB1", "sWC0", "sWC1",
              "sKH0", "sKH1", "sV0", "sV1", "sV2", "sXB0", "sXB1",
              "sCN0", "sCN1", "sCN2", "sCN3", "sST0", "sST1"):
        sems[s] = es.enter_context(nc.semaphore(s))

    ops = {e: [] for e in ("sync", "tensor", "vector", "scalar", "gpsimd")}
    cnt = {}

    def emit(eng, fn):
        ops[eng].append(fn)

    def inc(sem, by=1):
        cnt[sem] = cnt.get(sem, 0) + by
        return cnt[sem]

    def val(sem):
        return cnt.get(sem, 0)

    def W(eng, sem, v):
        assert v <= val(sem), f"wait {sem}>={v} but only {val(sem)} planned"
        if v <= 0:
            return
        emit(eng, lambda e, s=sem, vv=v: e.wait_ge(sems[s], vv))

    free_at = {}

    def use(eng, name):
        if name in free_at:
            s, v = free_at[name]
            W(eng, s, v)

    def release(name, sem, v=None):
        free_at[name] = (sem, v if v is not None else val(sem))

    def qdma(qeng, sem, fn):
        W(qeng, sem, val(sem))
        emit(qeng, fn)
        return inc(sem, 16)

    def g_fill(e):
        e.memset(identb[:], 0.0)
        e.affine_select(out=identb[:], in_=identb[:], compare_op=ALU.not_equal,
                        fill=1.0, base=0, pattern=[[-1, 128]], channel_multiplier=1)
        e.memset(identf[:], 0.0)
        e.affine_select(out=identf[:], in_=identf[:], compare_op=ALU.not_equal,
                        fill=1.0, base=0, pattern=[[-1, 128]], channel_multiplier=1)
        e.memset(onesb[:], 1.0)
        e.memset(onesf[:], 1.0).then_inc(sems["sPOOL"], 1)
    emit("gpsimd", g_fill)
    inc("sPOOL")
    emit("gpsimd", lambda e: e.dma_start(
        out=lutb[:, :], in_=dlut[None, :].broadcast_to([128, 16])).then_inc(sems["sG"], 16))
    inc("sG", 16)
    emit("gpsimd", lambda e: e.dma_start(
        out=brow[0:1, :], in_=dbias["bv"][None, :]).then_inc(sems["sG"], 16))
    inc("sG", 16)
    emit("gpsimd", lambda e: e.dma_start(
        out=brow[32:33, :], in_=dbias["bo"][None, :]).then_inc(sems["sG"], 16))
    inc("sG", 16)
    misc_g = val("sG")

    for bi, nm in enumerate(("bq", "bk")):
        sem = f"sX{bi}"
        qdma("sync", sem, lambda e, nm=nm, bi=bi, sem=sem: e.dma_start(
            out=x_stage[0][0:16, 128 * bi: 128 * (bi + 1)],
            in_=dbias[nm].rearrange("(a b) -> a b", b=128)).then_inc(sems[sem], 16))
    W("tensor", "sX0", 16)
    W("tensor", "sX1", 16)
    W("tensor", "sPOOL", 1)
    for bi in range(2):
        emit("tensor", lambda e, bi=bi: e.transpose(
            PB[4 + bi][:, 0:16], x_stage[0][0:16, 128 * bi: 128 * (bi + 1)],
            identf[0:16, 0:16]).then_inc(sems["sPE"], 1))
        inc("sPE")
    W("vector", "sPE", val("sPE"))
    emit("vector", lambda e: e.tensor_scalar_mul(bcol[:, 0:16], PB[4][:, 0:16], RS))
    emit("vector", lambda e: e.tensor_copy(bcol[:, 16:32], PB[5][:, 0:16]).then_inc(sems["sDVE"], 1))
    bcol_done = inc("sDVE")
    release("PB4", "sDVE", bcol_done)
    release("PB5", "sDVE", bcol_done)

    NFS = 3

    _fslots = [P2[0][:, 0:128], P2[0][:, 512:640], P2[1][:, 0:128]]

    def fslot(sl):
        return _fslots[sl]

    slot_free = [("sDVE", 0)] * NFS
    evac_after = [0] * 8
    for j in range(8):
        sem = f"sX{j % 2}"
        if j == 0:
            W("sync", "sPE", 2)
        if j >= 2:
            W("sync", "sDVE", evac_after[j - 2])
        ld = qdma("sync", sem, lambda e, j=j, sem=sem: e.dma_start(
            out=x_stage[j % 2], in_=dx[128 * j: 128 * (j + 1), :]).then_inc(sems[sem], 16))
        W("tensor", sem, ld)
        if j == 0:
            W("tensor", "sPOOL", 1)
        for k in range(16):
            sl = k % NFS
            W("tensor", slot_free[sl][0], slot_free[sl][1])
            emit("tensor", lambda e, j=j, k=k, sl=sl: e.transpose(
                fslot(sl), x_stage[j % 2][:, 128 * k: 128 * (k + 1)],
                identf[:]).then_inc(sems["sPE"], 1))
            tpc = inc("sPE")
            W("vector", "sPE", tpc)
            emit("vector", lambda e, j=j, k=k, sl=sl: e.tensor_copy(
                xs[:, T * k + 128 * j: T * k + 128 * (j + 1)], fslot(sl)).then_inc(sems["sDVE"], 1))
            slot_free[sl] = ("sDVE", inc("sDVE"))
        evac_after[j] = val("sDVE")
    for i in range(NFS):
        release(f"PB{i}", "sDVE", val("sDVE"))

    pt_slots = [P2[3][:, :].bitcast(BF16)[:, 0:128],
                P2[3][:, :].bitcast(BF16)[:, 1024:1152]]
    pt_n = [0]
    pt_free = [("sACT", 0), ("sACT", 0)]

    def pt_tr(src_ap, dst_ap, rows=128):
        sl = pt_n[0] % 2
        pt_n[0] += 1
        W("tensor", pt_free[sl][0], pt_free[sl][1])
        slot = pt_slots[sl]
        emit("tensor", lambda e, s=src_ap, d=slot: e.transpose(d[0:s.shape[1], :], s,
                                                               identb[:]).then_inc(sems["sPE"], 1))
        pc = inc("sPE")
        W("scalar", "sPE", pc)
        emit("scalar", lambda e, sl=sl, d=dst_ap, r=rows: e.copy(
            d, pt_slots[sl][0:r, :]).then_inc(sems["sACT"], 1))
        pt_free[sl] = ("sACT", inc("sACT"))
        return pt_free[sl][1]

    selst = {"free": [("sDVE", 0)] * 3, "bank": [("sDVE", 0), ("sDVE", 0)]}

    def emit_select_pe(dst, src, nrows, ncols):
        tring = (term, wsel[0], wsel[1]) if src is not dst else None
        tring = (term, term2, wsel2[0])
        banks = (P2[1][:, 512:1024], P2[2][:, 0:512])
        tcnt = {}
        for j in range(16):
            sl = j % 3
            W("vector", selst["free"][sl][0], selst["free"][sl][1])
            emit("vector", lambda e, j=j, sl=sl, tring=tring: e.tensor_scalar(
                out=tring[sl][0:nrows, 0:ncols], in0=src[0:nrows, 0:ncols],
                scalar1=float(j), scalar2=lutb[0:nrows, j: j + 1],
                op0=ALU.is_equal, op1=ALU.mult).then_inc(sems["sDVE"], 1))
            tcnt[j] = inc("sDVE")
            W("tensor", "sDVE", tcnt[j])
            if j == 0:
                use("tensor", "PB3")
                use("tensor", "PB4")
                for bi in range(2):
                    W("tensor", selst["bank"][bi][0], selst["bank"][bi][1])
            nh = (ncols + 511) // 512
            for bi in range(nh):
                wcols = min(512, ncols - 512 * bi)
                last = bi == nh - 1
                emit("tensor", lambda e, j=j, sl=sl, bi=bi, wcols=wcols, last=last,
                     tring=tring, banks=banks: (
                    e.matmul(banks[bi][0:nrows, 0:wcols], identb[0:nrows, 0:nrows],
                             tring[sl][0:nrows, 512 * bi: 512 * bi + wcols],
                             start=(j == 0), stop=(j == 15),
                             skip_group_check=True).then_inc(sems["sPE"], 1) if last else
                    e.matmul(banks[bi][0:nrows, 0:wcols], identb[0:nrows, 0:nrows],
                             tring[sl][0:nrows, 512 * bi: 512 * bi + wcols],
                             start=(j == 0), stop=(j == 15),
                             skip_group_check=True)))
            selst["free"][sl] = ("sPE", inc("sPE"))
        mmc = val("sPE")
        W("scalar", "sPE", mmc)
        nh = (ncols + 511) // 512
        for bi in range(nh):
            wcols = min(512, ncols - 512 * bi)
            last = bi == nh - 1
            emit("scalar", lambda e, bi=bi, wcols=wcols, last=last, banks=banks: (
                e.copy(dst[0:nrows, 512 * bi: 512 * bi + wcols],
                       banks[bi][0:nrows, 0:wcols]).then_inc(sems["sACT"], 1) if last else
                e.copy(dst[0:nrows, 512 * bi: 512 * bi + wcols],
                       banks[bi][0:nrows, 0:wcols])))
        ev = inc("sACT")
        selst["bank"] = [("sACT", ev), ("sACT", ev)]
        release("PB3", "sACT", ev)
        release("PB4", "sACT", ev)
        return ev

    def emit_select(dst, src, nrows, ncols, eng="vector", esem="sDVE"):
        return emit_select2(dst, src, nrows, ncols, eng, esem)

    def emit_select2(dst, src, nrows, ncols, eng, esem):
        emit(eng, lambda e: e.tensor_scalar(
            out=dst[0:nrows, 0:ncols], in0=src[0:nrows, 0:ncols], scalar1=0.0,
            scalar2=lutb[0:nrows, 0:1], op0=ALU.is_equal, op1=ALU.mult))
        tbuf = term if eng == "vector" else term2
        for j in range(1, 16):
            emit(eng, lambda e, j=j: e.tensor_scalar(
                out=tbuf[0:nrows, 0:ncols], in0=src[0:nrows, 0:ncols], scalar1=float(j),
                scalar2=lutb[0:nrows, j: j + 1], op0=ALU.is_equal, op1=ALU.mult))
            if j < 15:
                emit(eng, lambda e: e.tensor_tensor(
                    out=dst[0:nrows, 0:ncols], in0=dst[0:nrows, 0:ncols],
                    in1=tbuf[0:nrows, 0:ncols], op=ALU.add))
            else:
                emit(eng, lambda e: e.tensor_tensor(
                    out=dst[0:nrows, 0:ncols], in0=dst[0:nrows, 0:ncols],
                    in1=tbuf[0:nrows, 0:ncols], op=ALU.add).then_inc(sems[esem], 1))
        return inc(esem)

    def mat_gu(src, part):
        wsel_dma = {}
        nsel = [0]
        conv = {}
        tposed_pe = {}
        sel_of_span = {}
        W("sync", "sDVE", val("sDVE"))
        W("scalar", "sDVE", val("sDVE"))
        W("vector", "sPE", val("sPE"))
        for sp in range(6):
            ntiles = min(8, 43 - 8 * sp)
            pp = sp % 2
            nw = (ntiles + 3) // 4
            for w in range(nw):
                wt = min(4, ntiles - 4 * w)
                gj0 = 8 * sp + 4 * w
                widx = 2 * sp + w
                sem = f"sX{widx % 2}"
                if widx >= 2 and (widx - 2) in conv:
                    W("sync", "sDVE", conv[widx - 2])
                ld = qdma("sync", sem, lambda e, gj0=gj0, wt=wt, widx=widx, src=src, sem=sem: e.dma_start(
                    out=wave_i32[widx % 2][:, 0:256 * wt].rearrange("p (t h) -> p t h", t=wt),
                    in_=src[128 * gj0: 128 * (gj0 + wt), :].rearrange("(t p) h -> p t h", p=128)
                ).then_inc(sems[sem], 16))
                W("vector", sem, ld)
                if widx >= 2 and (widx - 2) in tposed_pe:
                    W("vector", "sPE", tposed_pe[widx - 2])
                emit("vector", lambda e, widx=widx, wt=wt: e.tensor_copy(
                    wave_bf[widx % 2][:, 0:256 * wt],
                    wave_i32[widx % 2][:, 0:256 * wt]).then_inc(sems["sDVE"], 1))
                conv[widx] = inc("sDVE")
                W("tensor", "sDVE", conv[widx])
                if sp >= 2 and w == 0 and (sp - 2) in sel_of_span:
                    W("scalar", "sDVE", sel_of_span[sp - 2])
                for tl in range(wt):
                    jl = 4 * w + tl
                    for hb in range(2):
                        pt_tr(wave_bf[widx % 2][:, 256 * tl + 128 * hb: 256 * tl + 128 * (hb + 1)],
                              idxT[pp][hb][:, 128 * jl: 128 * (jl + 1)])
                tposed_pe[widx] = val("sPE")
            W("vector", "sACT", val("sACT"))
            for hb in range(2):
                wb = nsel[0] % 2
                csem = f"sCN{wb}"
                if nsel[0] >= 2:
                    W("scalar", csem, wsel_dma[wb])
                sel = emit_select_pe(wsel[wb], idxT[pp][hb], 128, 128 * ntiles)
                W("gpsimd", "sACT", sel)
                wsel_dma[wb] = qdma("gpsimd", csem,
                                    lambda e, hb=hb, sp=sp, wb=wb, nt=ntiles, part=part, csem=csem: e.dma_start(
                    out=part[128 * hb: 128 * (hb + 1), 1024 * sp: 1024 * sp + 128 * nt],
                    in_=wsel[wb][:, 0: 128 * nt]).then_inc(sems[csem], 16))
                nsel[0] += 1
            sel_of_span[sp] = val("sDVE")

    def mat_d():
        wsel_dma = {}
        nsel = [0]
        conv = {}
        tposed_pe = {}
        sel_done = 0
        W("sync", "sDVE", val("sDVE"))
        W("scalar", "sDVE", val("sDVE"))
        W("vector", "sPE", val("sPE"))
        for dhalf in range(2):
            for jl in range(8):
                jg = 8 * dhalf + jl
                sem = f"sX{jg % 2}"
                if jg >= 2 and (jg - 2) in conv:
                    W("sync", "sDVE", conv[jg - 2])
                ld = qdma("sync", sem, lambda e, jg=jg, sem=sem: e.dma_start(
                    out=dstage_i32[jg % 2][:, 0:688],
                    in_=ddidx[128 * jg: 128 * (jg + 1), :]).then_inc(sems[sem], 16))
                W("vector", sem, ld)
                if jg >= 2 and (jg - 2) in tposed_pe:
                    W("vector", "sPE", tposed_pe[jg - 2])
                emit("vector", lambda e, jg=jg: e.tensor_copy(
                    dstage_bf[jg % 2][:, 0:688],
                    dstage_i32[jg % 2][:, 0:688]).then_inc(sems["sDVE"], 1))
                conv[jg] = inc("sDVE")
                W("tensor", "sDVE", conv[jg])
                if dhalf == 1 and jl == 0 and sel_done:
                    W("scalar", "sDVE", sel_done)
                for g in range(6):
                    gw = 128 if g < 5 else 48
                    pt_tr(dstage_bf[jg % 2][:, 128 * g: 128 * g + gw],
                          dT[g][0:gw, 128 * jl: 128 * (jl + 1)], rows=gw)
                tposed_pe[jg] = val("sPE")
            W("vector", "sACT", val("sACT"))
            for g in range(6):
                gw = 128 if g < 5 else 48
                wb = nsel[0] % 2
                csem = f"sCN{wb}"
                if nsel[0] >= 2:
                    W("vector", csem, wsel_dma[wb])
                sel = emit_select(wsel[wb], dT[g], gw, 1024)
                W("gpsimd", "sDVE", sel)
                wsel_dma[wb] = qdma("gpsimd", csem,
                                    lambda e, g=g, gw=gw, dhalf=dhalf, wb=wb, csem=csem: e.dma_start(
                    out=wd_part[128 * g: 128 * g + gw, 1024 * dhalf: 1024 * (dhalf + 1)],
                    in_=wsel[wb][0:gw, 0:1024]).then_inc(sems[csem], 16))
                nsel[0] += 1
            sel_done = val("sDVE")

    def mat_dense():
        W("sync", "sDVE", val("sDVE"))
        W("scalar", "sDVE", val("sDVE"))
        W("vector", "sPE", val("sPE"))
        conv = {}
        tposed_pe = {}
        for m, nm in enumerate(("wq", "wk", "wv", "wo")):
            for j16 in range(16):
                j = j16
                jg = 16 * m + j16
                slot_sem = f"sX{jg % 2}"
                if jg >= 2 and (jg - 2) in conv:
                    W("sync", "sDVE", conv[jg - 2])
                ld = qdma("sync", slot_sem, lambda e, nm=nm, j=j, jg=jg, slot_sem=slot_sem: e.dma_start(
                    out=den_stage[jg % 2][:, 0:256],
                    in_=ddense[nm][128 * j: 128 * (j + 1), :]).then_inc(sems[slot_sem], 16))
                W("vector", slot_sem, ld)
                if jg >= 2 and (jg - 2) in tposed_pe:
                    W("vector", "sPE", tposed_pe[jg - 2])
                emit("vector", lambda e, jg=jg: e.tensor_copy(
                    den_bf[jg % 2][:, 0:256], den_stage[jg % 2][:, 0:256]).then_inc(sems["sDVE"], 1))
                conv[jg] = inc("sDVE")
                W("tensor", "sDVE", conv[jg])
                if j == 0 and m > 0:
                    W("scalar", "sCN0", val("sCN0"))
                    W("scalar", "sCN1", val("sCN1"))
                for hb in range(2):
                    pt_tr(den_bf[jg % 2][:, 128 * hb: 128 * (hb + 1)],
                          denT[hb][:, 128 * j: 128 * (j + 1)])
                tposed_pe[jg] = val("sPE")
            W("gpsimd", "sACT", val("sACT"))
            for hb in range(2):
                csem = f"sCN{hb}"
                qdma("gpsimd", csem, lambda e, m=m, hb=hb, csem=csem: e.dma_start(
                    out=den_part[256 * m + 128 * hb: 256 * m + 128 * (hb + 1), :],
                    in_=denT[hb][:, :]).then_inc(sems[csem], 16))

    xb_n = [0]

    def xb_tr(dst_ap, src_ap):
        sem = f"sXB{xb_n[0] % 2}"
        xb_n[0] += 1
        W("scalar", sem, val(sem))
        emit("scalar", lambda e, d=dst_ap, s=src_ap, sem=sem: e.dma_start_transpose(
            d, s).then_inc(sems[sem], 16))
        return sem, inc(sem, 16)

    def mat_gu_pool(src, part, xin_pe):
        wsel_dma = {}
        nsel = [0]
        conv = {}
        xcnt = {}
        sel_of_span = {}
        W("sync", "sPE", xin_pe)
        for sp in range(6):
            ntiles = min(8, 43 - 8 * sp)
            pp = sp % 2
            nw = (ntiles + 3) // 4
            for w in range(nw):
                wt = min(4, ntiles - 4 * w)
                gj0 = 8 * sp + 4 * w
                widx = 2 * sp + w
                sem = f"sX{widx % 2}"
                if widx >= 2 and (widx - 2) in conv:
                    W("sync", "sPOOL", conv[widx - 2])
                ld = qdma("sync", sem, lambda e, gj0=gj0, wt=wt, widx=widx, src=src, sem=sem: e.dma_start(
                    out=wave2_i32[widx % 2][:, 0:256 * wt].rearrange("p (t h) -> p t h", t=wt),
                    in_=src[128 * gj0: 128 * (gj0 + wt), :].rearrange("(t p) h -> p t h", p=128)
                ).then_inc(sems[sem], 16))
                W("gpsimd", sem, ld)
                if widx >= 2 and (widx - 2) in xcnt:
                    s2, c2 = xcnt[widx - 2]
                    W("gpsimd", s2, c2)
                emit("gpsimd", lambda e, widx=widx, wt=wt: e.tensor_copy(
                    wave2_bf[widx % 2][:, 0:256 * wt],
                    wave2_i32[widx % 2][:, 0:256 * wt]).then_inc(sems["sPOOL"], 1))
                conv[widx] = inc("sPOOL")
                if sp >= 2 and w == 0 and (sp - 2) in sel_of_span:
                    pass
                for tl in range(wt):
                    jl = 4 * w + tl
                    for hb in range(2):
                        if tl == 0 and hb == 0:
                            W("scalar", "sPOOL", conv[widx])
                            if sp >= 2 and w == 0 and (sp - 2) in sel_of_span:
                                W("scalar", "sPOOL", sel_of_span[sp - 2])
                        xcnt[widx] = xb_tr(
                            idxT2[pp][hb][:, 128 * jl: 128 * (jl + 1)],
                            wave2_bf[widx % 2][:, 256 * tl + 128 * hb: 256 * tl + 128 * (hb + 1)])
            for hb in range(2):
                wb = nsel[0] % 2
                csem = f"sCN{wb}"
                s2, c2 = xcnt[2 * sp + nw - 1]
                W("gpsimd", s2, c2)
                W("gpsimd", f"sXB{0 if nw else 0}", val("sXB0"))
                W("gpsimd", "sXB1", val("sXB1"))
                if nsel[0] >= 2:
                    W("gpsimd", csem, wsel_dma[wb])
                sel = emit_select(wsel2[wb], idxT2[pp][hb], 128, 128 * ntiles,
                                  eng="gpsimd", esem="sPOOL")
                wsel_dma[wb] = qdma("gpsimd", csem,
                                    lambda e, hb=hb, sp=sp, wb=wb, nt=ntiles, part=part, csem=csem: e.dma_start(
                    out=part[128 * hb: 128 * (hb + 1), 1024 * sp: 1024 * sp + 128 * nt],
                    in_=wsel2[wb][:, 0: 128 * nt]).then_inc(sems[csem], 16))
                nsel[0] += 1
            sel_of_span[sp] = val("sPOOL")

    def mat_d_pool():
        wsel_dma = {}
        nsel = [0]
        conv = {}
        sel_done = 0
        W("sync", "sPOOL", val("sPOOL"))
        W("scalar", "sPOOL", val("sPOOL"))
        for dhalf in range(2):
            for jl in range(8):
                jg = 8 * dhalf + jl
                sem = f"sX{jg % 2}"
                if jg >= 2 and (jg - 2) in conv:
                    W("sync", "sPOOL", conv[jg - 2])
                ld = qdma("sync", sem, lambda e, jg=jg, sem=sem: e.dma_start(
                    out=dstage2_i32[jg % 2][:, 0:688],
                    in_=ddidx[128 * jg: 128 * (jg + 1), :]).then_inc(sems[sem], 16))
                W("gpsimd", sem, ld)
                if jg >= 2:
                    W("gpsimd", "sXB0", val("sXB0"))
                    W("gpsimd", "sXB1", val("sXB1"))
                emit("gpsimd", lambda e, jg=jg: e.tensor_copy(
                    dstage2_bf[jg % 2][:, 0:688],
                    dstage2_i32[jg % 2][:, 0:688]).then_inc(sems["sPOOL"], 1))
                conv[jg] = inc("sPOOL")
                emit("gpsimd", lambda e, jg=jg: e.memset(
                    dstage2_bf[jg % 2][:, 688:768], 0.0).then_inc(sems["sPOOL"], 1))
                inc("sPOOL")
                W("scalar", "sPOOL", val("sPOOL"))
                if dhalf == 1 and jl == 0 and sel_done:
                    W("scalar", "sPOOL", sel_done)
                for g in range(6):
                    if g < 5:
                        xb_tr(dT2[g][:, 128 * jl: 128 * (jl + 1)],
                              dstage2_bf[jg % 2][:, 128 * g: 128 * (g + 1)])
                    else:
                        xb_tr(dT2[5][:, 128 * jl: 128 * (jl + 1)],
                              dstage2_bf[jg % 2][:, 640: 768])
            W("gpsimd", "sXB0", val("sXB0"))
            W("gpsimd", "sXB1", val("sXB1"))
            for g in range(6):
                gw = 128 if g < 5 else 48
                wb = nsel[0] % 2
                csem = f"sCN{wb}"
                if nsel[0] >= 2:
                    W("gpsimd", csem, wsel_dma[wb])
                sel = emit_select(wsel2[wb], dT2[g], gw, 1024, eng="gpsimd", esem="sPOOL")
                wsel_dma[wb] = qdma("gpsimd", csem,
                                    lambda e, g=g, gw=gw, dhalf=dhalf, wb=wb, csem=csem: e.dma_start(
                    out=wd_part[128 * g: 128 * g + gw, 1024 * dhalf: 1024 * (dhalf + 1)],
                    in_=wsel2[wb][0:gw, 0:1024]).then_inc(sems[csem], 16))
                nsel[0] += 1
            sel_done = val("sPOOL")

    def mat_dense_pool():
        W("sync", "sPOOL", val("sPOOL"))
        W("scalar", "sPOOL", val("sPOOL"))
        conv = {}
        for m, nm in enumerate(("wq", "wk", "wv", "wo")):
            for j16 in range(16):
                j = j16
                jg = 16 * m + j16
                sem = f"sX{jg % 2}"
                if jg >= 2 and (jg - 2) in conv:
                    W("sync", "sPOOL", conv[jg - 2])
                ld = qdma("sync", sem, lambda e, nm=nm, j=j, jg=jg, sem=sem: e.dma_start(
                    out=den2_stage[jg % 2][:, 0:256],
                    in_=ddense[nm][128 * j: 128 * (j + 1), :]).then_inc(sems[sem], 16))
                W("gpsimd", sem, ld)
                if jg >= 2:
                    W("gpsimd", "sXB0", val("sXB0"))
                    W("gpsimd", "sXB1", val("sXB1"))
                emit("gpsimd", lambda e, jg=jg: e.tensor_copy(
                    den2_bf[jg % 2][:, 0:256], den2_stage[jg % 2][:, 0:256]
                ).then_inc(sems["sPOOL"], 1))
                conv[jg] = inc("sPOOL")
                if j == 0:
                    W("scalar", "sCN0", val("sCN0"))
                    W("scalar", "sCN1", val("sCN1"))
                W("scalar", "sPOOL", conv[jg])
                for hb in range(2):
                    xb_tr(denT2[hb][:, 128 * j: 128 * (j + 1)],
                          den2_bf[jg % 2][:, 128 * hb: 128 * (hb + 1)])
            W("gpsimd", "sXB0", val("sXB0"))
            W("gpsimd", "sXB1", val("sXB1"))
            for hb in range(2):
                csem = f"sCN{hb}"
                qdma("gpsimd", csem, lambda e, m=m, hb=hb, csem=csem: e.dma_start(
                    out=den_part[256 * m + 128 * hb: 256 * m + 128 * (hb + 1), :],
                    in_=denT2[hb][:, :]).then_inc(sems[csem], 16))

    def emit_cc(ins_ap, outs_ap, groups):
        W("gpsimd", "sG", val("sG"))
        for s in ("sCN0", "sCN1", "sCN2", "sCN3"):
            W("gpsimd", s, val(s))
        emit("gpsimd", lambda e: e.collective_compute(
            "AllGather", ALU.bypass, replica_groups=groups,
            ins=[ins_ap.opt()], outs=[outs_ap.opt()]).then_inc(sems["sCC"], 1))
        return inc("sCC")

    g8 = [list(range(8))]
    g4 = [[0, 1, 2, 3], [4, 5, 6, 7]]

    cc_g = cc_u = cc_d = cc_den = 0
    xin_pe = val("sPE")
    if STAGE >= -1:
        mat_gu(dgidx, wg_part)
        cc_g = emit_cc(wg_part, wg_all, g8)
    if STAGE >= 0:
        mat_gu(duidx, wu_part)
        cc_u = emit_cc(wu_part, wu_all, g8)
        mat_d()
        cc_d = emit_cc(wd_part, wd_all, g8)
        mat_dense_pool()
        cc_den = emit_cc(den_part, den_all, g8)
    release("PB4", "sACT", val("sACT"))
    release("PB5", "sACT", val("sACT"))

    if os.environ.get("BASS_MK_DUMPW"):
        dump_specs = [("dump_wg", wg_all, [D, II]), ("dump_wd", wd_all, [II, D]),
                      ("dump_den", den_all, [8192, D])]
        W("gpsimd", "sCC", cc_den)
        for nm, src_ap, shp in dump_specs:
            dap = nc.dram_tensor(nm, shp, BF16, kind="ExternalOutput").ap()
            emit("gpsimd", lambda e, dap=dap, src_ap=src_ap: e.dma_start(
                out=dap[:, :], in_=src_ap[:, :]).then_inc(sems["sG"], 16))
            inc("sG", 16)
        W("gpsimd", "sG", val("sG"))

    xl_free = [("sPE", 0)]

    def ph_ln(tile, dst=None):
        dst = xl if dst is None else dst
        use("vector", "xl")
        if dst is not xl:
            W("vector", "sPE", val("sPE"))
        for k in range(DC):
            last = k == DC - 1
            emit("vector", lambda e, k=k, tile=tile, last=last, dst=dst: (
                e.tensor_copy(dst[:, TT * k: TT * (k + 1)],
                              xs[:, T * k + TT * tile: T * k + TT * (tile + 1)]
                              ).then_inc(sems["sDVE"], 1) if last else
                e.tensor_copy(dst[:, TT * k: TT * (k + 1)],
                              xs[:, T * k + TT * tile: T * k + TT * (tile + 1)])))
        casts = inc("sDVE")
        W("scalar", "sDVE", casts)
        W("tensor", "sDVE", casts)
        use("tensor", "PR")
        mm_sq = {}
        sq_cnt = {}
        for k in range(DC):
            if k >= 2:
                W("scalar", "sPE", mm_sq[k - 2])
            emit("scalar", lambda e, k=k, dst=dst: e.activation(
                out=xsq[k % 2][:, :], in_=dst[:, TT * k: TT * (k + 1)],
                func=AF.Square).then_inc(sems["sACT"], 1))
            sq_cnt[k] = inc("sACT")
            emit("tensor", lambda e, k=k, dst=dst: e.matmul(
                PR[0:1, :], onesb[:, 0:1], dst[:, TT * k: TT * (k + 1)],
                start=(k == 0), stop=(k == DC - 1), skip_group_check=True))
            W("tensor", "sACT", sq_cnt[k])
            emit("tensor", lambda e, k=k: e.matmul(
                PR[32:33, :], onesb[:, 0:1], xsq[k % 2][:, :],
                start=(k == 0), stop=(k == DC - 1),
                skip_group_check=True).then_inc(sems["sPE"], 1))
            mm_sq[k] = inc("sPE")
        stats = mm_sq[DC - 1]
        W("scalar", "sPE", stats)
        emit("scalar", lambda e: e.activation(out=rowp[0:1, :], in_=PR[0:1, :],
                                              func=AF.Identity, scale=1.0 / D))
        emit("scalar", lambda e: e.activation(out=rowp2[0:1, :], in_=PR[32:33, :],
                                              func=AF.Identity,
                                              scale=1.0 / D).then_inc(sems["sACT"], 1))
        mrows = inc("sACT")
        release("PR", "sACT", mrows)
        W("vector", "sACT", mrows)
        emit("vector", lambda e: e.tensor_mul(tmpn[0:1, :], rowp[0:1, :], rowp[0:1, :]))
        emit("vector", lambda e: e.tensor_tensor(
            out=rowp2[0:1, :], in0=rowp2[0:1, :], in1=tmpn[0:1, :],
            op=ALU.subtract))
        emit("vector", lambda e: e.tensor_scalar_add(
            rowp2[0:1, :], rowp2[0:1, :], EPS).then_inc(sems["sDVE"], 1))
        varc = inc("sDVE")
        W("scalar", "sDVE", varc)
        emit("scalar", lambda e: e.activation(out=tmpn[0:1, :], in_=rowp2[0:1, :],
                                              func=AF.Ln))
        emit("scalar", lambda e: e.activation(out=rowp2[0:1, :], in_=tmpn[0:1, :],
                                              func=AF.Exp, scale=-0.5).then_inc(sems["sACT"], 1))
        rstdc = inc("sACT")
        W("vector", "sACT", rstdc)
        emit("vector", lambda e: e.scalar_tensor_tensor(
            out=tmpn[0:1, :], in0=rowp[0:1, :], scalar=-1.0, in1=rowp2[0:1, :],
            op0=ALU.mult, op1=ALU.mult).then_inc(sems["sDVE"], 1))
        brc = inc("sDVE")
        W("tensor", "sDVE", brc)
        use("tensor", "PB4")
        use("tensor", "PB5")
        emit("tensor", lambda e: e.matmul(PB[4][:, :], onesf[0:1, :], rowp2[0:1, :],
                                          start=True, stop=True, skip_group_check=True))
        emit("tensor", lambda e: e.matmul(PB[5][:, :], onesf[0:1, :], tmpn[0:1, :],
                                          start=True, stop=True,
                                          skip_group_check=True).then_inc(sems["sPE"], 1))
        bc = inc("sPE")
        W("scalar", "sPE", bc)
        emit("scalar", lambda e: e.copy(Abc[:, :], PB[4][:, :]))
        emit("scalar", lambda e: e.copy(Bbc[:, :], PB[5][:, :]).then_inc(sems["sACT"], 1))
        abc = inc("sACT")
        release("PB4", "sACT", abc)
        release("PB5", "sACT", abc)
        W("vector", "sACT", abc)
        W("vector", "sPE", stats)
        for k in range(DC):
            emit("vector", lambda e, k=k, tile=tile: e.tensor_tensor(
                out=tmpn[:, :], in0=xs[:, T * k + TT * tile: T * k + TT * (tile + 1)],
                in1=Abc[:, :], op=ALU.mult))
            last = k == DC - 1
            emit("vector", lambda e, k=k, last=last, dst=dst: (
                e.tensor_tensor(out=dst[:, TT * k: TT * (k + 1)], in0=tmpn[:, :],
                                in1=Bbc[:, :], op=ALU.add).then_inc(sems["sDVE"], 1) if last else
                e.tensor_tensor(out=dst[:, TT * k: TT * (k + 1)], in0=tmpn[:, :],
                                in1=Bbc[:, :], op=ALU.add)))
        return inc("sDVE")

    wring = {"n": 0, "consumed": [0, 0], "geo": "A"}
    setup_guard = [True]

    def wload(fn, two=False, geo="A"):
        if setup_guard[0]:
            W("sync", "sDVE", val("sDVE"))
            W("sync", "sACT", val("sACT"))
            W("sync", "sPE", val("sPE"))
            W("sync", "sG", val("sG"))
            setup_guard[0] = False
        b = wring["n"] % 2
        wring["n"] += 1
        if geo != wring["geo"]:
            W("sync", "sPE", max(wring["consumed"]))
            wring["geo"] = geo
        if wring["consumed"][b]:
            W("sync", "sPE", wring["consumed"][b])
        s1n = f"sWB{b}"
        W("sync", s1n, val(s1n))
        if two:
            s2n = f"sWC{b}"
            W("sync", s2n, val(s2n))
            emit("sync", fn(b))
            return b, ((s1n, inc(s1n, 16)), (s2n, inc(s2n, 16)))
        emit("sync", fn(b))
        return b, ((s1n, inc(s1n, 16)),)

    def ph_ffn(tile, ln_cnt):
        silu_cnt = {}
        up_ev = {}
        first = [True]
        last_xl_grp = [0]
        for phase, (wall, ccx) in enumerate(((wg_all, cc_g), (wu_all, cc_u))):
            for bi, (c0, w) in enumerate(FFN_BLOCKS):
                def mk(b, wall=wall, c0=c0, w=w):
                    def f(e):
                        e.dma_start(
                            out=wbuf[b][:, 0:16 * w].rearrange("p (kc o) -> p kc o", kc=16),
                            in_=wall.rearrange("(kc p) o -> p kc o", p=128)[:, :, c0:c0 + w],
                        ).then_inc(sems[f"sWB{b}"], 16)
                    return f
                b, ld = wload(mk, geo="A")
                for ocl in range(w // 128):
                    oc = 4 * bi + ocl
                    bank = 2 * phase + (oc % 2)
                    if ocl == 0:
                        for sname, scnt in ld:
                            W("tensor", sname, scnt)
                    if first[0]:
                        W("tensor", "sCC", ccx)
                        W("tensor", "sDVE", ln_cnt)
                        first[0] = False
                    if phase == 1 and oc < 2:
                        W("tensor", "sCC", cc_u)
                        use("tensor", f"PB{bank}")
                    if phase == 0 and oc < 2:
                        use("tensor", f"PB{bank}")
                    if phase == 0 and oc >= 2:
                        W("tensor", "sACT", silu_cnt[oc - 2])
                    if phase == 1 and oc >= 2:
                        W("tensor", "sDVE", up_ev[oc - 2])
                    for kc in range(16):
                        last = kc == 15
                        emit("tensor", lambda e, b=b, w=w, kc=kc, ocl=ocl, bank=bank, last=last: (
                            e.matmul(PB[bank][:, :],
                                     wbuf[b][:, w * kc + 128 * ocl: w * kc + 128 * (ocl + 1)],
                                     xl[:, TT * kc: TT * (kc + 1)], start=(kc == 0), stop=last,
                                     skip_group_check=True).then_inc(sems["sPE"], 1) if last else
                            e.matmul(PB[bank][:, :],
                                     wbuf[b][:, w * kc + 128 * ocl: w * kc + 128 * (ocl + 1)],
                                     xl[:, TT * kc: TT * (kc + 1)], start=(kc == 0), stop=last,
                                     skip_group_check=True)))
                    grp = inc("sPE")
                    if phase == 0:
                        W("scalar", "sPE", grp)
                        emit("scalar", lambda e, oc=oc, bank=bank: e.activation(
                            out=h_act[:, 512 * oc: 512 * (oc + 1)], in_=PB[bank][:, :],
                            func=AF.Silu).then_inc(sems["sACT"], 1))
                        silu_cnt[oc] = inc("sACT")
                    else:
                        W("vector", "sPE", grp)
                        W("vector", "sACT", silu_cnt[oc])
                        emit("vector", lambda e, oc=oc, bank=bank: e.tensor_tensor(
                            out=h_act[:, 512 * oc: 512 * (oc + 1)], in0=PB[bank][:, :],
                            in1=h_act[:, 512 * oc: 512 * (oc + 1)],
                            op=ALU.mult).then_inc(sems["sDVE"], 1))
                        up_ev[oc] = inc("sDVE")
                wring["consumed"][b] = val("sPE")
            last_xl_grp[0] = val("sPE")
        release("xl", "sPE", last_xl_grp[0])
        release("PB0", "sACT", silu_cnt[IC - 1])
        release("PB1", "sACT", silu_cnt[IC - 1])
        release("PB2", "sDVE", up_ev[IC - 1])
        release("PB3", "sDVE", up_ev[IC - 1])
        hact_done = up_ev[IC - 1]
        down_ev = {}
        for dc in range(DC):
            def mkd(b, dc=dc):
                def f(e):
                    e.dma_start(
                        out=wbuf[b][:, 0:II].rearrange("p (ic d) -> p ic d", ic=IC),
                        in_=wd_all.rearrange("(ic p) d -> p ic d", p=128)[:, :, 128 * dc: 128 * (dc + 1)],
                    ).then_inc(sems[f"sWB{b}"], 16)
                return f
            b, ld = wload(mkd, geo="A")
            bank = 4 + (dc % 2)
            for sname, scnt in ld:
                W("tensor", sname, scnt)
            if dc == 0:
                W("tensor", "sCC", cc_d)
                W("tensor", "sDVE", hact_done)
            if dc < 2:
                use("tensor", f"PB{bank}")
            if dc >= 2:
                W("tensor", "sDVE", down_ev[dc - 2])
            for ic in range(IC):
                last = ic == IC - 1
                emit("tensor", lambda e, b=b, ic=ic, bank=bank, last=last: (
                    e.matmul(PB[bank][:, :], wbuf[b][:, 128 * ic: 128 * (ic + 1)],
                             h_act[:, 512 * ic: 512 * (ic + 1)], start=(ic == 0), stop=last,
                             skip_group_check=True).then_inc(sems["sPE"], 1) if last else
                    e.matmul(PB[bank][:, :], wbuf[b][:, 128 * ic: 128 * (ic + 1)],
                             h_act[:, 512 * ic: 512 * (ic + 1)], start=(ic == 0), stop=last,
                             skip_group_check=True)))
            grp = inc("sPE")
            wring["consumed"][b] = grp
            W("vector", "sPE", grp)
            emit("vector", lambda e, dc=dc, tile=tile, bank=bank: e.scalar_tensor_tensor(
                out=xs[:, T * dc + TT * tile: T * dc + TT * (tile + 1)], in0=PB[bank][:, :],
                scalar=0.5, in1=xs[:, T * dc + TT * tile: T * dc + TT * (tile + 1)],
                op0=ALU.mult, op1=ALU.add).then_inc(sems["sDVE"], 1))
            down_ev[dc] = inc("sDVE")
        release("PB4", "sDVE", down_ev[DC - 1])
        release("PB5", "sDVE", down_ev[DC - 2])
        return val("sDVE")

    def dense_block_load256(m, db):
        def mk(b):
            def f(e):
                for par in range(2):
                    src_ = den_all.rearrange("(r m par q) d -> m par q r d",
                                             m=4, par=2, q=128)[m, par][:, :, 256 * db: 256 * (db + 1)]
                    dst = wbufq[b][:, 0:4096].rearrange("p (r par o) -> p par r o",
                                                        r=8, par=2)[:, par]
                    e.dma_start(out=dst, in_=src_).then_inc(
                        sems[f"sWB{b}" if par == 0 else f"sWC{b}"], 16)
            return f
        return wload(mk, two=True, geo="B")

    def qtile(b, kc, ocl):
        return wbufq[b][:, 512 * (kc // 2) + 256 * (kc % 2) + 128 * ocl:
                        512 * (kc // 2) + 256 * (kc % 2) + 128 * (ocl + 1)]

    def ph_qkv(ln_cnt):
        ks_dma = [0, 0]
        vs_dma = [0, 0]
        first = [True]
        xls = (xl, xl_t1)
        for m in (0, 1):
            ev = {}
            for db in range(8):
                b, ld = dense_block_load256(m, db)
                for ocl in range(2):
                    oc = 2 * db + ocl
                    banks = (oc % 2, 2 + oc % 2)
                    if ocl == 0:
                        for sname, scnt in ld:
                            W("tensor", sname, scnt)
                    if first[0]:
                        W("tensor", "sCC", cc_den)
                        W("tensor", "sDVE", ln_cnt)
                        first[0] = False
                    if oc < 2:
                        use("tensor", f"PB{banks[0]}")
                        use("tensor", f"PB{banks[1]}")
                    if oc >= 2:
                        W("tensor", "sACT", ev[(oc - 2, 1)])
                    for kc in range(16):
                        last = kc == 15
                        emit("tensor", lambda e, b=b, kc=kc, ocl=ocl, bk=banks[0]: e.matmul(
                            PB[bk], qtile(b, kc, ocl), xl[:, TT * kc: TT * (kc + 1)],
                            start=(kc == 0), stop=(kc == 15), skip_group_check=True))
                        emit("tensor", lambda e, b=b, kc=kc, ocl=ocl, bk=banks[1], last=last: (
                            e.matmul(PB[bk], qtile(b, kc, ocl), xl_t1[:, TT * kc: TT * (kc + 1)],
                                     start=(kc == 0), stop=(kc == 15),
                                     skip_group_check=True).then_inc(sems["sPE"], 1) if last else
                            e.matmul(PB[bk], qtile(b, kc, ocl), xl_t1[:, TT * kc: TT * (kc + 1)],
                                     start=(kc == 0), stop=(kc == 15), skip_group_check=True)))
                    grp = inc("sPE")
                    for tile in range(2):
                        bk = banks[tile]
                        W("scalar", "sPE", grp)
                        if m == 0:
                            emit("scalar", lambda e, oc=oc, tile=tile, bk=bk: e.activation(
                                out=Qb[:, T * oc + TT * tile: T * oc + TT * (tile + 1)],
                                in_=PB[bk], func=AF.Identity, scale=RS,
                                bias=bcol[:, oc: oc + 1]).then_inc(sems["sACT"], 1))
                            ev[(oc, tile)] = inc("sACT")
                        else:
                            ks = tile
                            csem = f"sCN{2 + ks}"
                            if ks_dma[ks]:
                                W("scalar", csem, ks_dma[ks])
                            emit("scalar", lambda e, oc=oc, ks=ks, bk=bk: e.activation(
                                out=kstage[ks][:, :], in_=PB[bk], func=AF.Identity,
                                bias=bcol[:, 16 + oc: 17 + oc]).then_inc(sems["sACT"], 1))
                            ev[(oc, tile)] = inc("sACT")
                            W("gpsimd", "sACT", ev[(oc, tile)])
                            ks_dma[ks] = qdma("gpsimd", csem,
                                              lambda e, oc=oc, ks=ks, tile=tile, csem=csem: e.dma_start(
                                out=kv_part[128 * oc: 128 * (oc + 1), TT * tile: TT * (tile + 1)],
                                in_=kstage[ks][:, :]).then_inc(sems[csem], 16))
                wring["consumed"][b] = val("sPE")
            release("PB0", "sACT", ev[(14, 0)])
            release("PB1", "sACT", ev[(15, 0)])
            release("PB2", "sACT", ev[(14, 1)])
            release("PB3", "sACT", ev[(15, 1)])
        cc_k = emit_cc(kv_part[0:2048, :], k_all, g4)
        v_view = kv_part[2048:4096, :].rearrange("(t two) c -> t (two c)", two=2)
        nv = [0]
        ev = {}
        for db in range(8):
            b, ld = dense_block_load256(2, db)
            for tile in range(2):
                for tcl in range(4):
                    bank = nv[0] % 2
                    if tile == 0 and tcl == 0:
                        for sname, scnt in ld:
                            W("tensor", sname, scnt)
                    if nv[0] < 2:
                        use("tensor", f"PB{bank}")
                        W("tensor", "sG", misc_g)
                    if nv[0] >= 2:
                        W("tensor", "sACT", ev[nv[0] - 2])
                    for kc in range(16):
                        emit("tensor", lambda e, b=b, kc=kc, tcl=tcl, tile=tile, bank=bank: e.matmul(
                            PB[bank][:, 0:256],
                            xls[tile][:, TT * kc + 128 * tcl: TT * kc + 128 * (tcl + 1)],
                            wbufq[b][:, 512 * (kc // 2) + 256 * (kc % 2):
                                     512 * (kc // 2) + 256 * (kc % 2) + 256],
                            start=(kc == 0), stop=False, skip_group_check=True))
                    emit("tensor", lambda e, db=db, bank=bank: e.matmul(
                        PB[bank][:, 0:256], onesb[0:1, 0:128], brow[0:1, 256 * db: 256 * (db + 1)],
                        start=False, stop=True, skip_group_check=True).then_inc(sems["sPE"], 1))
                    grp = inc("sPE")
                    W("scalar", "sPE", grp)
                    vs = nv[0] % 2
                    csem = f"sCN{2 + vs}"
                    if vs_dma[vs]:
                        W("scalar", csem, vs_dma[vs])
                    emit("scalar", lambda e, vs=vs, bank=bank: e.copy(
                        vstage[vs][:, 0:256], PB[bank][:, 0:256]).then_inc(sems["sACT"], 1))
                    ev[nv[0]] = inc("sACT")
                    W("gpsimd", "sACT", ev[nv[0]])
                    vs_dma[vs] = qdma("gpsimd", csem,
                                      lambda e, tile=tile, tcl=tcl, db=db, vs=vs, csem=csem: e.dma_start(
                        out=v_view[128 * (4 * tile + tcl): 128 * (4 * tile + tcl + 1),
                                   256 * db: 256 * (db + 1)],
                        in_=vstage[vs][:, 0:256]).then_inc(sems[csem], 16))
                    nv[0] += 1
            wring["consumed"][b] = val("sPE")
        release("xl", "sPE", val("sPE"))
        release("PB0", "sACT", ev[nv[0] - 2])
        release("PB1", "sACT", ev[nv[0] - 1])
        cc_v = emit_cc(kv_part[2048:4096, :], v_all, g4)
        return cc_k, cc_v

    def ph_attn(cc_k, cc_v):
        kld = [0, 0]
        kfree = [0, 0]
        vfree = [0, 0, 0]
        first_k = [True]
        at_free = [0]
        post_act = [0]
        AT = [P2[2], P2[3]]
        ROW = [(ti, bp, ch) for ti in range(4) for bp in (0, 32, 64) for ch in range(2)]

        for pair in range(8):
            h0, h1 = 2 * pair, 2 * pair + 1
            heads = ((0, h0), (1, h1))
            for hi, h in heads:
                if first_k[0] and hi == 0:
                    W("sync", "sCC", cc_k)
                    W("sync", "sPE", val("sPE"))
                    first_k[0] = False
                if kfree[hi]:
                    W("sync", "sPE", kfree[hi])
                ksem = f"sKH{hi}"
                kld[hi] = qdma("sync", ksem, lambda e, h=h, hi=hi, ksem=ksem: e.dma_start(
                    out=Kh[hi].rearrange("p (r c) -> p r c", r=4),
                    in_=k_all.rearrange("(r q) c -> q r c", r=4)[128 * h: 128 * (h + 1)]
                ).then_inc(sems[ksem], 16))

            sq_free = [0, 0]
            mm_cnt = {}
            first_row = [True]
            for hi, h in heads:
                for t in range(10):
                    sl = (10 * hi + t) % 2
                    if sq_free[sl]:
                        W("scalar", "sPE", sq_free[sl])
                    if t < 2:
                        emit("scalar", lambda e, h=h, t=t, sl=sl: e.activation(
                            out=sqs[sl][:, :], in_=Qb[:, T * h + TT * t: T * h + TT * (t + 1)],
                            func=AF.Square).then_inc(sems["sACT"], 1))
                    else:
                        if t == 2:
                            W("scalar", f"sKH{hi}", kld[hi])
                        emit("scalar", lambda e, hi=hi, t=t, sl=sl: e.activation(
                            out=sqs[sl][:, :], in_=Kh[hi][:, 512 * (t - 2): 512 * (t - 1)],
                            func=AF.Square).then_inc(sems["sACT"], 1))
                    sqc = inc("sACT")
                    rti, rbp, rch = ROW[10 * hi + t]
                    W("tensor", "sACT", sqc)
                    if first_row[0]:
                        if at_free[0]:
                            W("tensor", "sDVE", at_free[0])
                        if post_act[0]:
                            W("tensor", "sACT", post_act[0])
                        first_row[0] = False
                    emit("tensor", lambda e, rti=rti, rbp=rbp, rch=rch, sl=sl: e.matmul(
                        P2[rti][rbp: rbp + 1, 512 * rch: 512 * (rch + 1)],
                        onesb[:, 0:1], sqs[sl][:, :],
                        start=True, stop=True, skip_group_check=True).then_inc(sems["sPE"], 1))
                    mm_cnt[(hi, t)] = inc("sPE")
                    sq_free[sl] = mm_cnt[(hi, t)]
                W("vector", "sPE", mm_cnt[(hi, 9)])
                for t in range(10):
                    rti, rbp, rch = ROW[10 * hi + t]
                    emit("vector", lambda e, t=t, rti=rti, rbp=rbp, rch=rch: e.reduce_max(
                        crow[0:1, 4 + t: 5 + t],
                        P2[rti][rbp: rbp + 1, 512 * rch: 512 * (rch + 1)], axis=AX))
                emit("vector", lambda e: e.reduce_max(crow[0:1, 14:15], crow[0:1, 4:6], axis=AX))
                emit("vector", lambda e: e.reduce_max(crow[0:1, 15:16], crow[0:1, 6:14], axis=AX))
                emit("vector", lambda e: e.tensor_scalar_mul(
                    crow[0:1, 15:16], crow[0:1, 15:16], 1.0 / (2.0 * SQD)))
                emit("vector", lambda e: e.scalar_tensor_tensor(
                    out=crow[0:1, 14:15], in0=crow[0:1, 14:15], scalar=SQD / 2.0,
                    in1=crow[0:1, 15:16], op0=ALU.mult, op1=ALU.add))
                emit("vector", lambda e: e.tensor_scalar_mul(
                    crow[0:1, 14:15], crow[0:1, 14:15], -1.0).then_inc(sems["sDVE"], 1))
                cneg = inc("sDVE")
                W("tensor", "sDVE", cneg)
                emit("tensor", lambda e: e.matmul(
                    P2[0][:, 0:1], onesf[0:1, :], crow[0:1, 14:15],
                    start=True, stop=True, skip_group_check=True).then_inc(sems["sPE"], 1))
                bcc = inc("sPE")
                W("scalar", "sPE", bcc)
                emit("scalar", lambda e, hi=hi: e.copy(
                    crow[:, hi: hi + 1], P2[0][:, 0:1]).then_inc(sems["sACT"], 1))
                inc("sACT")
            cb_done = val("sACT")
            emit("vector", lambda e: e.memset(sacc4[0][:, :], 0.0))
            emit("vector", lambda e: e.memset(sacc4[1][:, :], 0.0).then_inc(sems["sDVE"], 1))
            sacc_d = inc("sDVE")
            if at_free[0]:
                W("gpsimd", "sDVE", at_free[0])
            emit("gpsimd", lambda e: e.memset(sacc4[2][:, :], 0.0))
            emit("gpsimd", lambda e: e.memset(sacc4[3][:, :], 0.0).then_inc(sems["sPOOL"], 1))
            sacc_p = inc("sPOOL")

            exp_cnt = [0, 0]
            av_pe = [0, 0]
            b1c = [0, 0]
            vldc = {}
            for j in range(33):
                if j < 32:
                    vb = j % 3
                    vsem = f"sV{vb}"
                    if vfree[vb]:
                        W("sync", "sPE", vfree[vb])
                    if j == 0 and pair == 0:
                        W("sync", "sCC", cc_v)
                    vld = qdma("sync", vsem, lambda e, j=j, vsem=vsem: e.dma_start(
                        out=Vb[j % 3][:, :],
                        in_=v_all[2048 * (j // 8) + 256 * (j % 8):
                                  2048 * (j // 8) + 256 * (j % 8 + 1), :]
                        .rearrange("(p two) c -> p (two c)", two=2)).then_inc(sems[vsem], 16))
                    vldc[j] = (vsem, vld)
                    for hh, h in heads:
                        if j == 0:
                            W("tensor", "sACT", cb_done)
                            W("tensor", f"sKH{hh}", kld[hh])
                        if exp_cnt[hh]:
                            W("tensor", "sACT", exp_cnt[hh])
                        emit("tensor", lambda e, hh=hh, h=h, j=j: e.matmul(
                            P2[hh][:, 0:512], Kh[hh][:, 128 * j: 128 * (j + 1)],
                            Qb[:, T * h: T * h + TT],
                            start=True, stop=True, skip_group_check=True))
                        emit("tensor", lambda e, hh=hh, h=h, j=j: e.matmul(
                            P2[hh][:, 512:1024], Kh[hh][:, 128 * j: 128 * (j + 1)],
                            Qb[:, T * h + TT: T * h + 2 * TT],
                            start=True, stop=True, skip_group_check=True).then_inc(sems["sPE"], 1))
                        b1c[hh] = inc("sPE")
                if j >= 1:
                    jj = j - 1
                    for hh, h in heads:
                        W("tensor", "sACT", exp_cnt[hh])
                        if hh == 0:
                            W("tensor", vldc[jj][0], vldc[jj][1])
                            if jj == 0 and at_free[0]:
                                W("tensor", "sDVE", at_free[0])
                        hcol = 128 * h
                        emit("tensor", lambda e, hh=hh, jj=jj, hcol=hcol: e.matmul(
                            AT[hh][:, 0:512], Vb[jj % 3][:, hcol: hcol + 128],
                            PTv[hh][:, 0:512],
                            start=(jj == 0), stop=(jj == 31), skip_group_check=True))
                        emit("tensor", lambda e, hh=hh, jj=jj, hcol=hcol: e.matmul(
                            AT[hh][:, 512:1024], Vb[jj % 3][:, hcol: hcol + 128],
                            PTv[hh][:, 512:1024],
                            start=(jj == 0), stop=(jj == 31),
                            skip_group_check=True).then_inc(sems["sPE"], 1))
                        av_pe[hh] = inc("sPE")
                    vfree[jj % 3] = av_pe[1]
                    W("vector", "sACT", exp_cnt[0])
                    emit("vector", lambda e: e.tensor_tensor(
                        out=sacc4[0][:, :], in0=sacc4[0][:, :], in1=PTv[0][:, 0:512],
                        op=ALU.add))
                    emit("vector", lambda e: e.tensor_tensor(
                        out=sacc4[1][:, :], in0=sacc4[1][:, :], in1=PTv[0][:, 512:1024],
                        op=ALU.add).then_inc(sems["sDVE"], 1))
                    sacc_d = inc("sDVE")
                    W("gpsimd", "sACT", exp_cnt[1])
                    emit("gpsimd", lambda e: e.tensor_tensor(
                        out=sacc4[2][:, :], in0=sacc4[2][:, :], in1=PTv[1][:, 0:512],
                        op=ALU.add))
                    emit("gpsimd", lambda e: e.tensor_tensor(
                        out=sacc4[3][:, :], in0=sacc4[3][:, :], in1=PTv[1][:, 512:1024],
                        op=ALU.add).then_inc(sems["sPOOL"], 1))
                    sacc_p = inc("sPOOL")
                if j < 32:
                    for hh, h in heads:
                        W("scalar", "sPE", b1c[hh])
                        if j > 0:
                            W("scalar", "sPE", av_pe[hh])
                            if hh == 0:
                                W("scalar", "sDVE", sacc_d)
                            else:
                                W("scalar", "sPOOL", sacc_p)
                        else:
                            if hh == 0:
                                W("scalar", "sDVE", sacc_d)
                            else:
                                W("scalar", "sPOOL", sacc_p)
                        emit("scalar", lambda e, hh=hh: e.activation(
                            out=PTv[hh][:, :], in_=P2[hh][:, :], func=AF.Exp,
                            bias=crow[:, hh: hh + 1]).then_inc(sems["sACT"], 1))
                        exp_cnt[hh] = inc("sACT")
            kfree[0] = b1c[0]
            kfree[1] = b1c[1]

            W("tensor", "sDVE", sacc_d)
            W("tensor", "sPOOL", sacc_p)
            W("tensor", "sACT", max(exp_cnt))
            rrows = [(0, 0, 0), (0, 32, 0), (0, 64, 0), (0, 0, 1)]
            for i in range(4):
                rti, rbp, rch = rrows[i]
                emit("tensor", lambda e, i=i, rti=rti, rbp=rbp, rch=rch: e.matmul(
                    P2[rti][rbp: rbp + 1, 512 * rch: 512 * (rch + 1)],
                    onesf[:, 0:1], sacc4[i][:, :],
                    start=True, stop=True,
                    skip_group_check=True).then_inc(sems["sPE"], 1) if i == 3 else e.matmul(
                    P2[rti][rbp: rbp + 1, 512 * rch: 512 * (rch + 1)],
                    onesf[:, 0:1], sacc4[i][:, :],
                    start=True, stop=True, skip_group_check=True))
            srow = inc("sPE")
            W("vector", "sPE", srow)
            emit("vector", lambda e: e.reciprocal(rowp2[0:1, :], P2[0][0:1, 0:512]))
            emit("vector", lambda e: e.reciprocal(rowp2[32:33, :], P2[0][32:33, 0:512]))
            emit("vector", lambda e: e.reciprocal(rowp2[64:65, :], P2[0][64:65, 0:512]))
            emit("vector", lambda e: e.reciprocal(
                rowp[0:1, :], P2[0][0:1, 512:1024]).then_inc(sems["sDVE"], 1))
            rc = inc("sDVE")
            rsrcs = [(rowp2[0:1, :], 0), (rowp2[32:33, :], 32),
                     (rowp2[64:65, :], 64), (rowp[0:1, :], 0)]
            ev_dve = {}
            bc_ev = {}
            for s in range(4):
                hh, sh = s // 2, s % 2
                rsrc, rb = rsrcs[s]
                bankv = P2[1][:, 0:512] if s % 2 == 0 else P2[1][:, 512:1024]
                bdst = Abc if s % 2 == 0 else Bbc
                W("tensor", "sDVE", rc)
                if s < 2:
                    W("tensor", "sACT", exp_cnt[1])
                else:
                    W("tensor", "sDVE", ev_dve[s - 2])
                emit("tensor", lambda e, rsrc=rsrc, rb=rb, bankv=bankv: e.matmul(
                    bankv, onesf[rb: rb + 1, :], rsrc, start=True, stop=True,
                    skip_group_check=True).then_inc(sems["sPE"], 1))
                bcx = inc("sPE")
                W("scalar", "sPE", bcx)
                emit("scalar", lambda e, bdst=bdst, bankv=bankv: e.copy(
                    bdst[:, :], bankv).then_inc(sems["sACT"], 1))
                bc_ev[s] = inc("sACT")
                W("vector", "sACT", bc_ev[s])
                W("vector", "sPE", av_pe[hh])
                h = (h0, h1)[hh]
                emit("vector", lambda e, hh=hh, sh=sh, h=h, bdst=bdst: e.tensor_tensor(
                    out=attn_s[:, T * h + TT * sh: T * h + TT * (sh + 1)],
                    in0=AT[hh][:, 512 * sh: 512 * (sh + 1)], in1=bdst[:, :],
                    op=ALU.mult).then_inc(sems["sDVE"], 1))
                ev_dve[s] = inc("sDVE")
            at_free[0] = val("sDVE")
            post_act[0] = val("sACT")
        release("PR", "sDVE", at_free[0])
        for i in range(6):
            release(f"PB{i}", "sDVE", at_free[0])
        return at_free[0]

    def ph_oproj(attn_done):
        oev = {}
        no = [0]
        W("sync", "sPE", val("sPE"))
        for db in range(8):
            b, ld = dense_block_load256(3, db)
            for dcl in range(2):
                dc = 2 * db + dcl
                banks = (no[0] % 2, 2 + no[0] % 2)
                if dcl == 0:
                    for sname, scnt in ld:
                        W("tensor", sname, scnt)
                if no[0] == 0:
                    W("tensor", "sDVE", attn_done)
                    W("tensor", "sG", misc_g)
                if no[0] < 2:
                    use("tensor", f"PB{banks[0]}")
                    use("tensor", f"PB{banks[1]}")
                if no[0] >= 2:
                    W("tensor", "sDVE", oev[(no[0] - 2, 1)])
                for oc in range(16):
                    emit("tensor", lambda e, b=b, oc=oc, dcl=dcl, bk=banks[0]: e.matmul(
                        PB[bk], qtile(b, oc, dcl),
                        attn_s[:, T * oc: T * oc + TT],
                        start=(oc == 0), stop=False, skip_group_check=True))
                    emit("tensor", lambda e, b=b, oc=oc, dcl=dcl, bk=banks[1]: e.matmul(
                        PB[bk], qtile(b, oc, dcl),
                        attn_s[:, T * oc + TT: T * oc + 2 * TT],
                        start=(oc == 0), stop=False, skip_group_check=True))
                emit("tensor", lambda e, dc=dc, bk=banks[0]: e.matmul(
                    PB[bk], brow[32:33, 128 * dc: 128 * (dc + 1)], onesb[32:33, :],
                    start=False, stop=True, skip_group_check=True))
                emit("tensor", lambda e, dc=dc, bk=banks[1]: e.matmul(
                    PB[bk], brow[32:33, 128 * dc: 128 * (dc + 1)], onesb[32:33, :],
                    start=False, stop=True, skip_group_check=True).then_inc(sems["sPE"], 1))
                grp = inc("sPE")
                for th in range(2):
                    bk = banks[th]
                    W("vector", "sPE", grp)
                    emit("vector", lambda e, dc=dc, th=th, bk=bk: e.tensor_tensor(
                        out=xs[:, T * dc + TT * th: T * dc + TT * (th + 1)], in0=PB[bk],
                        in1=xs[:, T * dc + TT * th: T * dc + TT * (th + 1)],
                        op=ALU.add).then_inc(sems["sDVE"], 1))
                    oev[(no[0], th)] = inc("sDVE")
                no[0] += 1
            wring["consumed"][b] = val("sPE")
        release("PB0", "sDVE", oev[(no[0] - 2, 0)])
        release("PB1", "sDVE", oev[(no[0] - 1, 0)])
        release("PB2", "sDVE", oev[(no[0] - 2, 1)])
        release("PB3", "sDVE", oev[(no[0] - 1, 1)])
        return val("sDVE")

    def ph_xout(prev_dve):
        W("tensor", "sDVE", prev_dve)
        use("tensor", "PB0")
        use("tensor", "PB1")
        use("tensor", "PB2")
        slot_free2 = [("sDVE", 0)] * NFS
        st_done = [0, 0]
        for j in range(8):
            if st_done[j % 2]:
                W("vector", f"sST{j % 2}", st_done[j % 2])
            for k in range(16):
                sl = k % NFS
                W("tensor", slot_free2[sl][0], slot_free2[sl][1])
                emit("tensor", lambda e, j=j, k=k, sl=sl: e.transpose(
                    fslot(sl), xs[:, T * k + 128 * j: T * k + 128 * (j + 1)],
                    identf[:]).then_inc(sems["sPE"], 1))
                tpc = inc("sPE")
                W("vector", "sPE", tpc)
                emit("vector", lambda e, j=j, k=k, sl=sl: e.tensor_copy(
                    ostage[j % 2][:, 128 * k: 128 * (k + 1)], fslot(sl)).then_inc(sems["sDVE"], 1))
                slot_free2[sl] = ("sDVE", inc("sDVE"))
            W("gpsimd", "sDVE", val("sDVE"))
            ssem = f"sST{j % 2}"
            st_done[j % 2] = qdma("gpsimd", ssem, lambda e, j=j, ssem=ssem: e.dma_start(
                out=dy[128 * j: 128 * (j + 1), :], in_=ostage[j % 2]).then_inc(sems[ssem], 16))
        W("gpsimd", "sST0", val("sST0"))
        W("gpsimd", "sST1", val("sST1"))

    if STAGE >= 1:
        for tile in range(2):
            lc = ph_ln(tile)
            ph_ffn(tile, lc)
    if STAGE >= 2:
        ph_ln(0)
        lc = ph_ln(1, xl_t1)
        cc_k, cc_v = ph_qkv(lc)
        adone = ph_attn(cc_k, cc_v)
        prev = ph_oproj(adone)
    else:
        prev = val("sDVE")
    if STAGE >= 3:
        for tile in range(2):
            lc = ph_ln(tile)
            ph_ffn(tile, lc)
        prev = val("sDVE")
    ph_xout(prev)

    with nc.Block() as block:
        @block.sync
        def _(e):
            for f in ops["sync"]:
                f(e)

        @block.tensor
        def _(e):
            for f in ops["tensor"]:
                f(e)

        @block.vector
        def _(e):
            for f in ops["vector"]:
                f(e)

        @block.scalar
        def _(e):
            for f in ops["scalar"]:
                f(e)

        @block.gpsimd
        def _(e):
            for f in ops["gpsimd"]:
                f(e)

    es.close()
    return nc


_CACHE = {}


def kernel(**inputs):
    x = np.asarray(inputs["L0_attn_hidden_states"], dtype=np.float32).reshape(8192, D)
    lut = np.ascontiguousarray(np.asarray(inputs["L0_ffn_lut"], dtype=np.float32))
    gidx = np.asarray(inputs["L0_ffn_gate_idx"], dtype=np.int32)
    uidx = np.asarray(inputs["L0_ffn_up_idx"], dtype=np.int32)
    didx = np.asarray(inputs["L0_ffn_down_idx"], dtype=np.int32)
    dense = {nm: np.asarray(inputs[f"L0_attn_{nm}"], dtype=np.float32)
             for nm in ("wq", "wk", "wv", "wo")}
    bias = {nm: np.ascontiguousarray(np.asarray(inputs[f"L0_attn_{nm}"], dtype=np.float32))
            for nm in ("bq", "bk", "bv", "bo")}

    if "nc" not in _CACHE:
        _CACHE["nc"] = build_nc()
    nc = _CACHE["nc"]

    in_maps = []
    for c in range(NCORES):
        m = {
            "x": np.ascontiguousarray(x[T * c: T * (c + 1)]),
            "gidx": np.ascontiguousarray(gidx[:, 256 * c: 256 * (c + 1)]),
            "uidx": np.ascontiguousarray(uidx[:, 256 * c: 256 * (c + 1)]),
            "didx": np.ascontiguousarray(didx[:, 688 * c: 688 * (c + 1)]),
            "lut": lut,
        }
        for nm in ("wq", "wk", "wv", "wo"):
            m[nm] = np.ascontiguousarray(dense[nm][:, 256 * c: 256 * (c + 1)])
        for nm in ("bq", "bk", "bv", "bo"):
            m[nm] = bias[nm]
        in_maps.append(m)

    res = run_bass_kernel_spmd(nc, in_maps, core_ids=list(range(NCORES)))
    _CACHE["last_res"] = res
    y = np.concatenate([res.results[c]["y"] for c in range(NCORES)], axis=0)
    return y.reshape(2, 4096, D).astype(np.float32)


# version 06 (final)
# speedup vs baseline: 1.0342x; 1.0342x over previous
import math
import os
from contextlib import ExitStack

import numpy as np

import concourse.bass as bass
import concourse.mybir as mybir
from concourse.bass_utils import run_bass_kernel_spmd

F32 = mybir.dt.float32
BF16 = mybir.dt.bfloat16
I32 = mybir.dt.int32
AF = mybir.ActivationFunctionType
ALU = mybir.AluOpType
AX = mybir.AxisListType.X

NCORES = 8
T = 1024
TT = 512
D = 2048
DC = 16
II = 5504
IC = 43
H = 16
EPS = 1e-5
RS = 1.0 / math.sqrt(128.0)
SQD = math.sqrt(128.0)

STAGE = int(os.environ.get("BASS_MK_STAGE", "3"))

FFN_BLOCKS = [(c, min(512, II - c)) for c in range(0, II, 512)]


def build_nc():
    nc = bass.Bass("TRN2", target_bir_lowering=False, detect_race_conditions=False)

    dx = nc.dram_tensor("x", [T, D], F32, kind="ExternalInput").ap()
    dgidx = nc.dram_tensor("gidx", [II, 256], I32, kind="ExternalInput").ap()
    duidx = nc.dram_tensor("uidx", [II, 256], I32, kind="ExternalInput").ap()
    ddidx = nc.dram_tensor("didx", [D, 688], I32, kind="ExternalInput").ap()
    dlut = nc.dram_tensor("lut", [16], F32, kind="ExternalInput").ap()
    ddense = {nm: nc.dram_tensor(nm, [D, 256], F32, kind="ExternalInput").ap()
              for nm in ("wq", "wk", "wv", "wo")}
    dbias = {nm: nc.dram_tensor(nm, [D], F32, kind="ExternalInput").ap()
             for nm in ("bq", "bk", "bv", "bo")}
    dy = nc.dram_tensor("y", [T, D], F32, kind="ExternalOutput").ap()

    wg_part = nc.dram_tensor("wg_part", [256, II], BF16, kind="Internal").ap()
    wu_part = nc.dram_tensor("wu_part", [256, II], BF16, kind="Internal").ap()
    wd_part = nc.dram_tensor("wd_part", [688, D], BF16, kind="Internal").ap()
    den_part = nc.dram_tensor("den_part", [1024, D], BF16, kind="Internal").ap()
    kv_part = nc.dram_tensor("kv_part", [4096, T], BF16, kind="Internal").ap()
    wg_all = nc.dram_tensor("wg_all", [D, II], BF16, kind="Internal").ap()
    wu_all = nc.dram_tensor("wu_all", [D, II], BF16, kind="Internal").ap()
    wd_all = nc.dram_tensor("wd_all", [II, D], BF16, kind="Internal").ap()
    den_all = nc.dram_tensor("den_all", [8192, D], BF16, kind="Internal").ap()
    kv_all = nc.dram_tensor("kv_all", [16384, T], BF16, kind="Internal").ap()
    k_all = kv_all[0:8192, :]
    v_all = kv_all[8192:16384, :]

    es = ExitStack()

    def sb(name, shape, dtype):
        return es.enter_context(nc.sbuf_tensor(name, shape, dtype))

    xs = sb("xs", [128, DC * T], F32)
    xl = sb("xl", [128, DC * TT], BF16)
    RA = sb("RA", [128, 32768], BF16)
    RB = sb("RB", [128, 23552], BF16)
    Abc = sb("Abc", [128, 512], F32)
    Bbc = sb("Bbc", [128, 512], F32)
    tmpn = sb("tmpn", [128, 512], F32)
    rowp = sb("rowp", [128, 512], F32)
    rowp2 = sb("rowp2", [128, 512], F32)
    identb = sb("identb", [128, 128], BF16)
    identf = sb("identf", [128, 128], F32)
    onesb = sb("onesb", [128, 512], BF16)
    onesf = sb("onesf", [128, 128], F32)
    brow = sb("brow", [64, D], BF16)
    bcol = sb("bcol", [128, 32], F32)
    lutb = sb("lutb", [128, 16], F32)
    crow = sb("crow", [128, 16], F32)

    x_stage = [RA[:, 16384 + 4096 * i: 16384 + 4096 * (i + 1)].bitcast(F32) for i in range(2)]
    wave_i32 = [RA[:, 2048 * i: 2048 * (i + 1)].bitcast(I32) for i in range(2)]
    wave_bf = [RA[:, 4096 + 1024 * i: 4096 + 1024 * (i + 1)] for i in range(2)]
    idxT = [[RA[:, 6144 + 1024 * (2 * pp + hb): 6144 + 1024 * (2 * pp + hb + 1)]
             for hb in range(2)] for pp in range(2)]
    wsel = [RA[:, 10368 + 1024 * i: 10368 + 1024 * (i + 1)] for i in range(2)]
    term = RA[:, 12416: 13440]
    dstage_i32 = [RA[:, 1376 * i: 1376 * (i + 1)].bitcast(I32) for i in range(2)]
    dstage_bf = [RA[:, 2752 + 688 * i: 2752 + 688 * (i + 1)] for i in range(2)]
    dT = [RA[:, 4128 + 1024 * g: 4128 + 1024 * (g + 1)] for g in range(6)]
    den_stage = [RA[:, 512 * i: 512 * (i + 1)].bitcast(F32) for i in range(2)]
    den_bf = [RA[:, 1024 + 256 * i: 1024 + 256 * (i + 1)] for i in range(2)]
    denT = [RA[:, 4096 + 2048 * hb: 4096 + 2048 * (hb + 1)] for hb in range(2)]
    wbuf = [RA[:, 8192 * i: 8192 * (i + 1)] for i in range(2)]
    wbufq = [RA[:, 4096 * i: 4096 * (i + 1)] for i in range(2)]
    xl_t1 = RA[:, 8192: 16384]
    Qb = RA[:, 16384: 32768]
    wave2_i32 = [RA[:, 16384 + 2048 * i: 16384 + 2048 * (i + 1)].bitcast(I32) for i in range(2)]
    wave2_bf = [RA[:, 20480 + 1024 * i: 20480 + 1024 * (i + 1)] for i in range(2)]
    idxT2 = [[RA[:, 22528 + 1024 * (2 * pp + hb): 22528 + 1024 * (2 * pp + hb + 1)]
              for hb in range(2)] for pp in range(2)]
    wsel2 = [RA[:, 28672 + 1024 * i: 28672 + 1024 * (i + 1)] for i in range(2)]
    term2 = RA[:, 30720: 31744]
    dstage2_i32 = [RA[:, 16384 + 1536 * i: 16384 + 1536 * (i + 1)].bitcast(I32) for i in range(2)]
    dstage2_bf = [RA[:, 19456 + 1536 * i: 19456 + 1536 * (i + 1)] for i in range(2)]
    dT2 = [RA[:, 22528 + 1024 * g: 22528 + 1024 * (g + 1)] for g in range(6)]
    den2_stage = [RA[:, 16384 + 512 * i: 16384 + 512 * (i + 1)].bitcast(F32) for i in range(2)]
    den2_bf = [RA[:, 17408 + 512 * i: 17408 + 512 * (i + 1)] for i in range(2)]
    denT2 = [RA[:, 18432 + 2048 * hb: 18432 + 2048 * (hb + 1)] for hb in range(2)]
    Kh = [RA[:, 4096 * i: 4096 * (i + 1)] for i in range(2)]
    Vb = [RA[:, 8192 + 2048 * i: 8192 + 2048 * (i + 1)] for i in range(3)]
    ostage = [RA[:, 4096 * i: 4096 * (i + 1)].bitcast(F32) for i in range(2)]

    h_act = RB[:, 0: IC * 512]
    xsq = [RB[:, 22016 + 512 * i: 22016 + 512 * (i + 1)] for i in range(2)]
    attn_s = RB[:, 0: 16384]
    PTv = [RB[:, 16384 + 1024 * hh: 16384 + 1024 * (hh + 1)] for hh in range(2)]
    sacc4 = [RB[:, 18432 + 1024 * i: 18432 + 1024 * (i + 1)].bitcast(F32)
             for i in range(4)]
    sqs = [RB[:, 22528 + 512 * i: 22528 + 512 * (i + 1)] for i in range(2)]
    kstage = [RB[:, 512 * i: 512 * (i + 1)] for i in range(2)]
    vstage = [RB[:, 1024 + 512 * i: 1024 + 512 * (i + 1)] for i in range(2)]

    def psum(name, shape, dtype=F32):
        return es.enter_context(nc.psum_tensor(name, shape, dtype))

    P2 = [psum(f"P2_{i}", [128, 1024], F32) for i in range(4)]
    PB = [P2[0][:, 0:512], P2[0][:, 512:1024], P2[1][:, 0:512], P2[1][:, 512:1024],
          P2[2][:, 0:512], P2[2][:, 512:1024]]
    PR = P2[3][:, 0:512]

    sems = {}
    for s in ("sG", "sCC", "sPE", "sACT", "sDVE", "sPOOL",
              "sX0", "sX1", "sWB0", "sWB1", "sWC0", "sWC1",
              "sKH0", "sKH1", "sV0", "sV1", "sV2", "sXB0", "sXB1",
              "sCN0", "sCN1", "sCN2", "sCN3", "sST0", "sST1"):
        sems[s] = es.enter_context(nc.semaphore(s))

    ops = {e: [] for e in ("sync", "tensor", "vector", "scalar", "gpsimd")}
    cnt = {}

    def emit(eng, fn):
        ops[eng].append(fn)

    def inc(sem, by=1):
        cnt[sem] = cnt.get(sem, 0) + by
        return cnt[sem]

    def val(sem):
        return cnt.get(sem, 0)

    def W(eng, sem, v):
        assert v <= val(sem), f"wait {sem}>={v} but only {val(sem)} planned"
        if v <= 0:
            return
        emit(eng, lambda e, s=sem, vv=v: e.wait_ge(sems[s], vv))

    free_at = {}

    def use(eng, name):
        if name in free_at:
            s, v = free_at[name]
            W(eng, s, v)

    def release(name, sem, v=None):
        free_at[name] = (sem, v if v is not None else val(sem))

    def qdma(qeng, sem, fn):
        W(qeng, sem, val(sem))
        emit(qeng, fn)
        return inc(sem, 16)

    def g_fill(e):
        e.memset(identb[:], 0.0)
        e.affine_select(out=identb[:], in_=identb[:], compare_op=ALU.not_equal,
                        fill=1.0, base=0, pattern=[[-1, 128]], channel_multiplier=1)
        e.memset(identf[:], 0.0)
        e.affine_select(out=identf[:], in_=identf[:], compare_op=ALU.not_equal,
                        fill=1.0, base=0, pattern=[[-1, 128]], channel_multiplier=1)
        e.memset(onesb[:], 1.0)
        e.memset(onesf[:], 1.0).then_inc(sems["sPOOL"], 1)
    emit("gpsimd", g_fill)
    inc("sPOOL")
    emit("gpsimd", lambda e: e.dma_start(
        out=lutb[:, :], in_=dlut[None, :].broadcast_to([128, 16])).then_inc(sems["sG"], 16))
    inc("sG", 16)
    emit("gpsimd", lambda e: e.dma_start(
        out=brow[0:1, :], in_=dbias["bv"][None, :]).then_inc(sems["sG"], 16))
    inc("sG", 16)
    emit("gpsimd", lambda e: e.dma_start(
        out=brow[32:33, :], in_=dbias["bo"][None, :]).then_inc(sems["sG"], 16))
    inc("sG", 16)
    misc_g = val("sG")

    for bi, nm in enumerate(("bq", "bk")):
        sem = f"sX{bi}"
        qdma("sync", sem, lambda e, nm=nm, bi=bi, sem=sem: e.dma_start(
            out=x_stage[0][0:16, 128 * bi: 128 * (bi + 1)],
            in_=dbias[nm].rearrange("(a b) -> a b", b=128)).then_inc(sems[sem], 16))
    W("tensor", "sX0", 16)
    W("tensor", "sX1", 16)
    W("tensor", "sPOOL", 1)
    for bi in range(2):
        emit("tensor", lambda e, bi=bi: e.transpose(
            PB[4 + bi][:, 0:16], x_stage[0][0:16, 128 * bi: 128 * (bi + 1)],
            identf[0:16, 0:16]).then_inc(sems["sPE"], 1))
        inc("sPE")
    W("vector", "sPE", val("sPE"))
    emit("vector", lambda e: e.tensor_scalar_mul(bcol[:, 0:16], PB[4][:, 0:16], RS))
    emit("vector", lambda e: e.tensor_copy(bcol[:, 16:32], PB[5][:, 0:16]).then_inc(sems["sDVE"], 1))
    bcol_done = inc("sDVE")
    release("PB4", "sDVE", bcol_done)
    release("PB5", "sDVE", bcol_done)

    NFS = 3

    _fslots = [P2[0][:, 0:128], P2[0][:, 512:640], P2[1][:, 0:128]]

    def fslot(sl):
        return _fslots[sl]

    slot_free = [("sDVE", 0)] * NFS
    evac_after = [0] * 8
    for j in range(8):
        sem = f"sX{j % 2}"
        if j == 0:
            W("sync", "sPE", 2)
        if j >= 2:
            W("sync", "sDVE", evac_after[j - 2])
        ld = qdma("sync", sem, lambda e, j=j, sem=sem: e.dma_start(
            out=x_stage[j % 2], in_=dx[128 * j: 128 * (j + 1), :]).then_inc(sems[sem], 16))
        W("tensor", sem, ld)
        if j == 0:
            W("tensor", "sPOOL", 1)
        for k in range(16):
            sl = k % NFS
            W("tensor", slot_free[sl][0], slot_free[sl][1])
            emit("tensor", lambda e, j=j, k=k, sl=sl: e.transpose(
                fslot(sl), x_stage[j % 2][:, 128 * k: 128 * (k + 1)],
                identf[:]).then_inc(sems["sPE"], 1))
            tpc = inc("sPE")
            W("vector", "sPE", tpc)
            emit("vector", lambda e, j=j, k=k, sl=sl: e.tensor_copy(
                xs[:, T * k + 128 * j: T * k + 128 * (j + 1)], fslot(sl)).then_inc(sems["sDVE"], 1))
            slot_free[sl] = ("sDVE", inc("sDVE"))
        evac_after[j] = val("sDVE")
    for i in range(NFS):
        release(f"PB{i}", "sDVE", val("sDVE"))

    pt_slots = [P2[3][:, :].bitcast(BF16)[:, 0:128],
                P2[3][:, :].bitcast(BF16)[:, 1024:1152]]
    pt_n = [0]
    pt_free = [("sACT", 0), ("sACT", 0)]

    def pt_tr(src_ap, dst_ap, rows=128):
        sl = pt_n[0] % 2
        pt_n[0] += 1
        W("tensor", pt_free[sl][0], pt_free[sl][1])
        slot = pt_slots[sl]
        emit("tensor", lambda e, s=src_ap, d=slot: e.transpose(d[0:s.shape[1], :], s,
                                                               identb[:]).then_inc(sems["sPE"], 1))
        pc = inc("sPE")
        W("scalar", "sPE", pc)
        emit("scalar", lambda e, sl=sl, d=dst_ap, r=rows: e.copy(
            d, pt_slots[sl][0:r, :]).then_inc(sems["sACT"], 1))
        pt_free[sl] = ("sACT", inc("sACT"))
        return pt_free[sl][1]

    selst = {"free": [("sDVE", 0)] * 3, "bank": [("sDVE", 0), ("sDVE", 0)]}

    def emit_select_pe(dst, src, nrows, ncols):
        tring = (term, wsel[0], wsel[1]) if src is not dst else None
        tring = (term, term2, wsel2[0])
        banks = (P2[1][:, 512:1024], P2[2][:, 0:512])
        tcnt = {}
        for j in range(16):
            sl = j % 3
            W("vector", selst["free"][sl][0], selst["free"][sl][1])
            emit("vector", lambda e, j=j, sl=sl, tring=tring: e.tensor_scalar(
                out=tring[sl][0:nrows, 0:ncols], in0=src[0:nrows, 0:ncols],
                scalar1=float(j), scalar2=lutb[0:nrows, j: j + 1],
                op0=ALU.is_equal, op1=ALU.mult).then_inc(sems["sDVE"], 1))
            tcnt[j] = inc("sDVE")
            W("tensor", "sDVE", tcnt[j])
            if j == 0:
                use("tensor", "PB3")
                use("tensor", "PB4")
                for bi in range(2):
                    W("tensor", selst["bank"][bi][0], selst["bank"][bi][1])
            nh = (ncols + 511) // 512
            for bi in range(nh):
                wcols = min(512, ncols - 512 * bi)
                last = bi == nh - 1
                emit("tensor", lambda e, j=j, sl=sl, bi=bi, wcols=wcols, last=last,
                     tring=tring, banks=banks: (
                    e.matmul(banks[bi][0:nrows, 0:wcols], identb[0:nrows, 0:nrows],
                             tring[sl][0:nrows, 512 * bi: 512 * bi + wcols],
                             start=(j == 0), stop=(j == 15),
                             skip_group_check=True).then_inc(sems["sPE"], 1) if last else
                    e.matmul(banks[bi][0:nrows, 0:wcols], identb[0:nrows, 0:nrows],
                             tring[sl][0:nrows, 512 * bi: 512 * bi + wcols],
                             start=(j == 0), stop=(j == 15),
                             skip_group_check=True)))
            selst["free"][sl] = ("sPE", inc("sPE"))
        mmc = val("sPE")
        W("scalar", "sPE", mmc)
        nh = (ncols + 511) // 512
        for bi in range(nh):
            wcols = min(512, ncols - 512 * bi)
            last = bi == nh - 1
            emit("scalar", lambda e, bi=bi, wcols=wcols, last=last, banks=banks: (
                e.copy(dst[0:nrows, 512 * bi: 512 * bi + wcols],
                       banks[bi][0:nrows, 0:wcols]).then_inc(sems["sACT"], 1) if last else
                e.copy(dst[0:nrows, 512 * bi: 512 * bi + wcols],
                       banks[bi][0:nrows, 0:wcols])))
        ev = inc("sACT")
        selst["bank"] = [("sACT", ev), ("sACT", ev)]
        release("PB3", "sACT", ev)
        release("PB4", "sACT", ev)
        return ev

    def emit_select(dst, src, nrows, ncols, eng="vector", esem="sDVE"):
        return emit_select2(dst, src, nrows, ncols, eng, esem)

    def emit_select2(dst, src, nrows, ncols, eng, esem):
        emit(eng, lambda e: e.tensor_scalar(
            out=dst[0:nrows, 0:ncols], in0=src[0:nrows, 0:ncols], scalar1=0.0,
            scalar2=lutb[0:nrows, 0:1], op0=ALU.is_equal, op1=ALU.mult))
        tbuf = term if eng == "vector" else term2
        for j in range(1, 16):
            emit(eng, lambda e, j=j: e.tensor_scalar(
                out=tbuf[0:nrows, 0:ncols], in0=src[0:nrows, 0:ncols], scalar1=float(j),
                scalar2=lutb[0:nrows, j: j + 1], op0=ALU.is_equal, op1=ALU.mult))
            if j < 15:
                emit(eng, lambda e: e.tensor_tensor(
                    out=dst[0:nrows, 0:ncols], in0=dst[0:nrows, 0:ncols],
                    in1=tbuf[0:nrows, 0:ncols], op=ALU.add))
            else:
                emit(eng, lambda e: e.tensor_tensor(
                    out=dst[0:nrows, 0:ncols], in0=dst[0:nrows, 0:ncols],
                    in1=tbuf[0:nrows, 0:ncols], op=ALU.add).then_inc(sems[esem], 1))
        return inc(esem)

    def mat_gu(src, part):
        wsel_dma = {}
        nsel = [0]
        conv = {}
        tposed_pe = {}
        sel_of_span = {}
        W("sync", "sDVE", val("sDVE"))
        W("scalar", "sDVE", val("sDVE"))
        W("vector", "sPE", val("sPE"))
        for sp in range(6):
            ntiles = min(8, 43 - 8 * sp)
            pp = sp % 2
            nw = (ntiles + 3) // 4
            for w in range(nw):
                wt = min(4, ntiles - 4 * w)
                gj0 = 8 * sp + 4 * w
                widx = 2 * sp + w
                sem = f"sX{widx % 2}"
                if widx >= 2 and (widx - 2) in conv:
                    W("sync", "sDVE", conv[widx - 2])
                ld = qdma("sync", sem, lambda e, gj0=gj0, wt=wt, widx=widx, src=src, sem=sem: e.dma_start(
                    out=wave_i32[widx % 2][:, 0:256 * wt].rearrange("p (t h) -> p t h", t=wt),
                    in_=src[128 * gj0: 128 * (gj0 + wt), :].rearrange("(t p) h -> p t h", p=128)
                ).then_inc(sems[sem], 16))
                W("vector", sem, ld)
                if widx >= 2 and (widx - 2) in tposed_pe:
                    W("vector", "sPE", tposed_pe[widx - 2])
                emit("vector", lambda e, widx=widx, wt=wt: e.tensor_copy(
                    wave_bf[widx % 2][:, 0:256 * wt],
                    wave_i32[widx % 2][:, 0:256 * wt]).then_inc(sems["sDVE"], 1))
                conv[widx] = inc("sDVE")
                W("tensor", "sDVE", conv[widx])
                if sp >= 2 and w == 0 and (sp - 2) in sel_of_span:
                    W("scalar", "sDVE", sel_of_span[sp - 2])
                for tl in range(wt):
                    jl = 4 * w + tl
                    for hb in range(2):
                        pt_tr(wave_bf[widx % 2][:, 256 * tl + 128 * hb: 256 * tl + 128 * (hb + 1)],
                              idxT[pp][hb][:, 128 * jl: 128 * (jl + 1)])
                tposed_pe[widx] = val("sPE")
            W("vector", "sACT", val("sACT"))
            for hb in range(2):
                wb = nsel[0] % 2
                csem = f"sCN{wb}"
                if nsel[0] >= 2:
                    W("scalar", csem, wsel_dma[wb])
                sel = emit_select_pe(wsel[wb], idxT[pp][hb], 128, 128 * ntiles)
                W("gpsimd", "sACT", sel)
                wsel_dma[wb] = qdma("gpsimd", csem,
                                    lambda e, hb=hb, sp=sp, wb=wb, nt=ntiles, part=part, csem=csem: e.dma_start(
                    out=part[128 * hb: 128 * (hb + 1), 1024 * sp: 1024 * sp + 128 * nt],
                    in_=wsel[wb][:, 0: 128 * nt]).then_inc(sems[csem], 16))
                nsel[0] += 1
            sel_of_span[sp] = val("sDVE")

    def mat_d():
        wsel_dma = {}
        nsel = [0]
        conv = {}
        tposed_pe = {}
        sel_done = 0
        W("sync", "sDVE", val("sDVE"))
        W("scalar", "sDVE", val("sDVE"))
        W("vector", "sPE", val("sPE"))
        for dhalf in range(2):
            for jl in range(8):
                jg = 8 * dhalf + jl
                sem = f"sX{jg % 2}"
                if jg >= 2 and (jg - 2) in conv:
                    W("sync", "sDVE", conv[jg - 2])
                ld = qdma("sync", sem, lambda e, jg=jg, sem=sem: e.dma_start(
                    out=dstage_i32[jg % 2][:, 0:688],
                    in_=ddidx[128 * jg: 128 * (jg + 1), :]).then_inc(sems[sem], 16))
                W("vector", sem, ld)
                if jg >= 2 and (jg - 2) in tposed_pe:
                    W("vector", "sPE", tposed_pe[jg - 2])
                emit("vector", lambda e, jg=jg: e.tensor_copy(
                    dstage_bf[jg % 2][:, 0:688],
                    dstage_i32[jg % 2][:, 0:688]).then_inc(sems["sDVE"], 1))
                conv[jg] = inc("sDVE")
                W("tensor", "sDVE", conv[jg])
                if dhalf == 1 and jl == 0 and sel_done:
                    W("scalar", "sDVE", sel_done)
                for g in range(6):
                    gw = 128 if g < 5 else 48
                    pt_tr(dstage_bf[jg % 2][:, 128 * g: 128 * g + gw],
                          dT[g][0:gw, 128 * jl: 128 * (jl + 1)], rows=gw)
                tposed_pe[jg] = val("sPE")
            W("vector", "sACT", val("sACT"))
            for g in range(6):
                gw = 128 if g < 5 else 48
                wb = nsel[0] % 2
                csem = f"sCN{wb}"
                if nsel[0] >= 2:
                    W("vector", csem, wsel_dma[wb])
                sel = emit_select(wsel[wb], dT[g], gw, 1024)
                W("gpsimd", "sDVE", sel)
                wsel_dma[wb] = qdma("gpsimd", csem,
                                    lambda e, g=g, gw=gw, dhalf=dhalf, wb=wb, csem=csem: e.dma_start(
                    out=wd_part[128 * g: 128 * g + gw, 1024 * dhalf: 1024 * (dhalf + 1)],
                    in_=wsel[wb][0:gw, 0:1024]).then_inc(sems[csem], 16))
                nsel[0] += 1
            sel_done = val("sDVE")

    def mat_dense():
        W("sync", "sDVE", val("sDVE"))
        W("scalar", "sDVE", val("sDVE"))
        W("vector", "sPE", val("sPE"))
        conv = {}
        tposed_pe = {}
        for m, nm in enumerate(("wq", "wk", "wv", "wo")):
            for j16 in range(16):
                j = j16
                jg = 16 * m + j16
                slot_sem = f"sX{jg % 2}"
                if jg >= 2 and (jg - 2) in conv:
                    W("sync", "sDVE", conv[jg - 2])
                ld = qdma("sync", slot_sem, lambda e, nm=nm, j=j, jg=jg, slot_sem=slot_sem: e.dma_start(
                    out=den_stage[jg % 2][:, 0:256],
                    in_=ddense[nm][128 * j: 128 * (j + 1), :]).then_inc(sems[slot_sem], 16))
                W("vector", slot_sem, ld)
                if jg >= 2 and (jg - 2) in tposed_pe:
                    W("vector", "sPE", tposed_pe[jg - 2])
                emit("vector", lambda e, jg=jg: e.tensor_copy(
                    den_bf[jg % 2][:, 0:256], den_stage[jg % 2][:, 0:256]).then_inc(sems["sDVE"], 1))
                conv[jg] = inc("sDVE")
                W("tensor", "sDVE", conv[jg])
                if j == 0 and m > 0:
                    W("scalar", "sCN0", val("sCN0"))
                    W("scalar", "sCN1", val("sCN1"))
                for hb in range(2):
                    pt_tr(den_bf[jg % 2][:, 128 * hb: 128 * (hb + 1)],
                          denT[hb][:, 128 * j: 128 * (j + 1)])
                tposed_pe[jg] = val("sPE")
            W("gpsimd", "sACT", val("sACT"))
            for hb in range(2):
                csem = f"sCN{hb}"
                qdma("gpsimd", csem, lambda e, m=m, hb=hb, csem=csem: e.dma_start(
                    out=den_part[256 * m + 128 * hb: 256 * m + 128 * (hb + 1), :],
                    in_=denT[hb][:, :]).then_inc(sems[csem], 16))

    xb_n = [0]

    def xb_tr(dst_ap, src_ap):
        sem = f"sXB{xb_n[0] % 2}"
        xb_n[0] += 1
        W("scalar", sem, val(sem))
        emit("scalar", lambda e, d=dst_ap, s=src_ap, sem=sem: e.dma_start_transpose(
            d, s).then_inc(sems[sem], 16))
        return sem, inc(sem, 16)

    def mat_gu_pool(src, part, xin_pe):
        wsel_dma = {}
        nsel = [0]
        conv = {}
        xcnt = {}
        sel_of_span = {}
        W("sync", "sPE", xin_pe)
        for sp in range(6):
            ntiles = min(8, 43 - 8 * sp)
            pp = sp % 2
            nw = (ntiles + 3) // 4
            for w in range(nw):
                wt = min(4, ntiles - 4 * w)
                gj0 = 8 * sp + 4 * w
                widx = 2 * sp + w
                sem = f"sX{widx % 2}"
                if widx >= 2 and (widx - 2) in conv:
                    W("sync", "sPOOL", conv[widx - 2])
                ld = qdma("sync", sem, lambda e, gj0=gj0, wt=wt, widx=widx, src=src, sem=sem: e.dma_start(
                    out=wave2_i32[widx % 2][:, 0:256 * wt].rearrange("p (t h) -> p t h", t=wt),
                    in_=src[128 * gj0: 128 * (gj0 + wt), :].rearrange("(t p) h -> p t h", p=128)
                ).then_inc(sems[sem], 16))
                W("gpsimd", sem, ld)
                if widx >= 2 and (widx - 2) in xcnt:
                    s2, c2 = xcnt[widx - 2]
                    W("gpsimd", s2, c2)
                emit("gpsimd", lambda e, widx=widx, wt=wt: e.tensor_copy(
                    wave2_bf[widx % 2][:, 0:256 * wt],
                    wave2_i32[widx % 2][:, 0:256 * wt]).then_inc(sems["sPOOL"], 1))
                conv[widx] = inc("sPOOL")
                if sp >= 2 and w == 0 and (sp - 2) in sel_of_span:
                    pass
                for tl in range(wt):
                    jl = 4 * w + tl
                    for hb in range(2):
                        if tl == 0 and hb == 0:
                            W("scalar", "sPOOL", conv[widx])
                            if sp >= 2 and w == 0 and (sp - 2) in sel_of_span:
                                W("scalar", "sPOOL", sel_of_span[sp - 2])
                        xcnt[widx] = xb_tr(
                            idxT2[pp][hb][:, 128 * jl: 128 * (jl + 1)],
                            wave2_bf[widx % 2][:, 256 * tl + 128 * hb: 256 * tl + 128 * (hb + 1)])
            for hb in range(2):
                wb = nsel[0] % 2
                csem = f"sCN{wb}"
                s2, c2 = xcnt[2 * sp + nw - 1]
                W("gpsimd", s2, c2)
                W("gpsimd", f"sXB{0 if nw else 0}", val("sXB0"))
                W("gpsimd", "sXB1", val("sXB1"))
                if nsel[0] >= 2:
                    W("gpsimd", csem, wsel_dma[wb])
                sel = emit_select(wsel2[wb], idxT2[pp][hb], 128, 128 * ntiles,
                                  eng="gpsimd", esem="sPOOL")
                wsel_dma[wb] = qdma("gpsimd", csem,
                                    lambda e, hb=hb, sp=sp, wb=wb, nt=ntiles, part=part, csem=csem: e.dma_start(
                    out=part[128 * hb: 128 * (hb + 1), 1024 * sp: 1024 * sp + 128 * nt],
                    in_=wsel2[wb][:, 0: 128 * nt]).then_inc(sems[csem], 16))
                nsel[0] += 1
            sel_of_span[sp] = val("sPOOL")

    def mat_d_pool():
        wsel_dma = {}
        nsel = [0]
        conv = {}
        sel_done = 0
        W("sync", "sPOOL", val("sPOOL"))
        W("scalar", "sPOOL", val("sPOOL"))
        for dhalf in range(2):
            for jl in range(8):
                jg = 8 * dhalf + jl
                sem = f"sX{jg % 2}"
                if jg >= 2 and (jg - 2) in conv:
                    W("sync", "sPOOL", conv[jg - 2])
                ld = qdma("sync", sem, lambda e, jg=jg, sem=sem: e.dma_start(
                    out=dstage2_i32[jg % 2][:, 0:688],
                    in_=ddidx[128 * jg: 128 * (jg + 1), :]).then_inc(sems[sem], 16))
                W("gpsimd", sem, ld)
                if jg >= 2:
                    W("gpsimd", "sXB0", val("sXB0"))
                    W("gpsimd", "sXB1", val("sXB1"))
                emit("gpsimd", lambda e, jg=jg: e.tensor_copy(
                    dstage2_bf[jg % 2][:, 0:688],
                    dstage2_i32[jg % 2][:, 0:688]).then_inc(sems["sPOOL"], 1))
                conv[jg] = inc("sPOOL")
                emit("gpsimd", lambda e, jg=jg: e.memset(
                    dstage2_bf[jg % 2][:, 688:768], 0.0).then_inc(sems["sPOOL"], 1))
                inc("sPOOL")
                W("scalar", "sPOOL", val("sPOOL"))
                if dhalf == 1 and jl == 0 and sel_done:
                    W("scalar", "sPOOL", sel_done)
                for g in range(6):
                    if g < 5:
                        xb_tr(dT2[g][:, 128 * jl: 128 * (jl + 1)],
                              dstage2_bf[jg % 2][:, 128 * g: 128 * (g + 1)])
                    else:
                        xb_tr(dT2[5][:, 128 * jl: 128 * (jl + 1)],
                              dstage2_bf[jg % 2][:, 640: 768])
            W("gpsimd", "sXB0", val("sXB0"))
            W("gpsimd", "sXB1", val("sXB1"))
            for g in range(6):
                gw = 128 if g < 5 else 48
                wb = nsel[0] % 2
                csem = f"sCN{wb}"
                if nsel[0] >= 2:
                    W("gpsimd", csem, wsel_dma[wb])
                sel = emit_select(wsel2[wb], dT2[g], gw, 1024, eng="gpsimd", esem="sPOOL")
                wsel_dma[wb] = qdma("gpsimd", csem,
                                    lambda e, g=g, gw=gw, dhalf=dhalf, wb=wb, csem=csem: e.dma_start(
                    out=wd_part[128 * g: 128 * g + gw, 1024 * dhalf: 1024 * (dhalf + 1)],
                    in_=wsel2[wb][0:gw, 0:1024]).then_inc(sems[csem], 16))
                nsel[0] += 1
            sel_done = val("sPOOL")

    def mat_dense_pool():
        W("sync", "sPOOL", val("sPOOL"))
        W("scalar", "sPOOL", val("sPOOL"))
        conv = {}
        for m, nm in enumerate(("wq", "wk", "wv", "wo")):
            for j16 in range(16):
                j = j16
                jg = 16 * m + j16
                sem = f"sX{jg % 2}"
                if jg >= 2 and (jg - 2) in conv:
                    W("sync", "sPOOL", conv[jg - 2])
                ld = qdma("sync", sem, lambda e, nm=nm, j=j, jg=jg, sem=sem: e.dma_start(
                    out=den2_stage[jg % 2][:, 0:256],
                    in_=ddense[nm][128 * j: 128 * (j + 1), :]).then_inc(sems[sem], 16))
                W("gpsimd", sem, ld)
                if jg >= 2:
                    W("gpsimd", "sXB0", val("sXB0"))
                    W("gpsimd", "sXB1", val("sXB1"))
                emit("gpsimd", lambda e, jg=jg: e.tensor_copy(
                    den2_bf[jg % 2][:, 0:256], den2_stage[jg % 2][:, 0:256]
                ).then_inc(sems["sPOOL"], 1))
                conv[jg] = inc("sPOOL")
                if j == 0:
                    W("scalar", "sCN0", val("sCN0"))
                    W("scalar", "sCN1", val("sCN1"))
                W("scalar", "sPOOL", conv[jg])
                for hb in range(2):
                    xb_tr(denT2[hb][:, 128 * j: 128 * (j + 1)],
                          den2_bf[jg % 2][:, 128 * hb: 128 * (hb + 1)])
            W("gpsimd", "sXB0", val("sXB0"))
            W("gpsimd", "sXB1", val("sXB1"))
            for hb in range(2):
                csem = f"sCN{hb}"
                qdma("gpsimd", csem, lambda e, m=m, hb=hb, csem=csem: e.dma_start(
                    out=den_part[256 * m + 128 * hb: 256 * m + 128 * (hb + 1), :],
                    in_=denT2[hb][:, :]).then_inc(sems[csem], 16))

    def emit_cc(ins_ap, outs_ap, groups):
        W("gpsimd", "sG", val("sG"))
        for s in ("sCN0", "sCN1", "sCN2", "sCN3"):
            W("gpsimd", s, val(s))
        emit("gpsimd", lambda e: e.collective_compute(
            "AllGather", ALU.bypass, replica_groups=groups,
            ins=[ins_ap.opt()], outs=[outs_ap.opt()]).then_inc(sems["sCC"], 1))
        return inc("sCC")

    g8 = [list(range(8))]
    g4 = [[0, 1, 2, 3], [4, 5, 6, 7]]

    cc_g = cc_u = cc_d = cc_den = 0
    xin_pe = val("sPE")
    if STAGE >= -1:
        mat_gu(dgidx, wg_part)
        cc_g = emit_cc(wg_part, wg_all, g8)
    if STAGE >= 0:
        mat_gu(duidx, wu_part)
        cc_u = emit_cc(wu_part, wu_all, g8)
        mat_d()
        cc_d = emit_cc(wd_part, wd_all, g8)
        mat_dense()
        cc_den = emit_cc(den_part, den_all, g8)
    release("PB4", "sACT", val("sACT"))
    release("PB5", "sACT", val("sACT"))

    if os.environ.get("BASS_MK_DUMPW"):
        dump_specs = [("dump_wg", wg_all, [D, II]), ("dump_wd", wd_all, [II, D]),
                      ("dump_den", den_all, [8192, D])]
        W("gpsimd", "sCC", cc_den)
        for nm, src_ap, shp in dump_specs:
            dap = nc.dram_tensor(nm, shp, BF16, kind="ExternalOutput").ap()
            emit("gpsimd", lambda e, dap=dap, src_ap=src_ap: e.dma_start(
                out=dap[:, :], in_=src_ap[:, :]).then_inc(sems["sG"], 16))
            inc("sG", 16)
        W("gpsimd", "sG", val("sG"))

    xl_free = [("sPE", 0)]

    def ph_ln(tile, dst=None):
        dst = xl if dst is None else dst
        use("vector", "xl")
        if dst is not xl:
            W("vector", "sPE", val("sPE"))
        for k in range(DC):
            last = k == DC - 1
            emit("vector", lambda e, k=k, tile=tile, last=last, dst=dst: (
                e.tensor_copy(dst[:, TT * k: TT * (k + 1)],
                              xs[:, T * k + TT * tile: T * k + TT * (tile + 1)]
                              ).then_inc(sems["sDVE"], 1) if last else
                e.tensor_copy(dst[:, TT * k: TT * (k + 1)],
                              xs[:, T * k + TT * tile: T * k + TT * (tile + 1)])))
        casts = inc("sDVE")
        W("scalar", "sDVE", casts)
        W("tensor", "sDVE", casts)
        use("tensor", "PR")
        mm_sq = {}
        sq_cnt = {}
        for k in range(DC):
            if k >= 2:
                W("scalar", "sPE", mm_sq[k - 2])
            emit("scalar", lambda e, k=k, dst=dst: e.activation(
                out=xsq[k % 2][:, :], in_=dst[:, TT * k: TT * (k + 1)],
                func=AF.Square).then_inc(sems["sACT"], 1))
            sq_cnt[k] = inc("sACT")
            emit("tensor", lambda e, k=k, dst=dst: e.matmul(
                PR[0:1, :], onesb[:, 0:1], dst[:, TT * k: TT * (k + 1)],
                start=(k == 0), stop=(k == DC - 1), skip_group_check=True))
            W("tensor", "sACT", sq_cnt[k])
            emit("tensor", lambda e, k=k: e.matmul(
                PR[32:33, :], onesb[:, 0:1], xsq[k % 2][:, :],
                start=(k == 0), stop=(k == DC - 1),
                skip_group_check=True).then_inc(sems["sPE"], 1))
            mm_sq[k] = inc("sPE")
        stats = mm_sq[DC - 1]
        W("scalar", "sPE", stats)
        emit("scalar", lambda e: e.activation(out=rowp[0:1, :], in_=PR[0:1, :],
                                              func=AF.Identity, scale=1.0 / D))
        emit("scalar", lambda e: e.activation(out=rowp2[0:1, :], in_=PR[32:33, :],
                                              func=AF.Identity,
                                              scale=1.0 / D).then_inc(sems["sACT"], 1))
        mrows = inc("sACT")
        release("PR", "sACT", mrows)
        W("vector", "sACT", mrows)
        emit("vector", lambda e: e.tensor_mul(tmpn[0:1, :], rowp[0:1, :], rowp[0:1, :]))
        emit("vector", lambda e: e.tensor_tensor(
            out=rowp2[0:1, :], in0=rowp2[0:1, :], in1=tmpn[0:1, :],
            op=ALU.subtract))
        emit("vector", lambda e: e.tensor_scalar_add(
            rowp2[0:1, :], rowp2[0:1, :], EPS).then_inc(sems["sDVE"], 1))
        varc = inc("sDVE")
        W("scalar", "sDVE", varc)
        emit("scalar", lambda e: e.activation(out=tmpn[0:1, :], in_=rowp2[0:1, :],
                                              func=AF.Ln))
        emit("scalar", lambda e: e.activation(out=rowp2[0:1, :], in_=tmpn[0:1, :],
                                              func=AF.Exp, scale=-0.5).then_inc(sems["sACT"], 1))
        rstdc = inc("sACT")
        W("vector", "sACT", rstdc)
        emit("vector", lambda e: e.scalar_tensor_tensor(
            out=tmpn[0:1, :], in0=rowp[0:1, :], scalar=-1.0, in1=rowp2[0:1, :],
            op0=ALU.mult, op1=ALU.mult).then_inc(sems["sDVE"], 1))
        brc = inc("sDVE")
        W("tensor", "sDVE", brc)
        use("tensor", "PB4")
        use("tensor", "PB5")
        emit("tensor", lambda e: e.matmul(PB[4][:, :], onesf[0:1, :], rowp2[0:1, :],
                                          start=True, stop=True, skip_group_check=True))
        emit("tensor", lambda e: e.matmul(PB[5][:, :], onesf[0:1, :], tmpn[0:1, :],
                                          start=True, stop=True,
                                          skip_group_check=True).then_inc(sems["sPE"], 1))
        bc = inc("sPE")
        W("scalar", "sPE", bc)
        emit("scalar", lambda e: e.copy(Abc[:, :], PB[4][:, :]))
        emit("scalar", lambda e: e.copy(Bbc[:, :], PB[5][:, :]).then_inc(sems["sACT"], 1))
        abc = inc("sACT")
        release("PB4", "sACT", abc)
        release("PB5", "sACT", abc)
        W("vector", "sACT", abc)
        W("vector", "sPE", stats)
        for k in range(DC):
            emit("vector", lambda e, k=k, tile=tile: e.tensor_tensor(
                out=tmpn[:, :], in0=xs[:, T * k + TT * tile: T * k + TT * (tile + 1)],
                in1=Abc[:, :], op=ALU.mult))
            last = k == DC - 1
            emit("vector", lambda e, k=k, last=last, dst=dst: (
                e.tensor_tensor(out=dst[:, TT * k: TT * (k + 1)], in0=tmpn[:, :],
                                in1=Bbc[:, :], op=ALU.add).then_inc(sems["sDVE"], 1) if last else
                e.tensor_tensor(out=dst[:, TT * k: TT * (k + 1)], in0=tmpn[:, :],
                                in1=Bbc[:, :], op=ALU.add)))
        return inc("sDVE")

    wring = {"n": 0, "consumed": [0, 0], "geo": "A"}
    setup_guard = [True]

    def wload(fn, two=False, geo="A"):
        if setup_guard[0]:
            W("sync", "sDVE", val("sDVE"))
            W("sync", "sACT", val("sACT"))
            W("sync", "sPE", val("sPE"))
            W("sync", "sG", val("sG"))
            setup_guard[0] = False
        b = wring["n"] % 2
        wring["n"] += 1
        if geo != wring["geo"]:
            W("sync", "sPE", max(wring["consumed"]))
            wring["geo"] = geo
        if wring["consumed"][b]:
            W("sync", "sPE", wring["consumed"][b])
        s1n = f"sWB{b}"
        W("sync", s1n, val(s1n))
        if two:
            s2n = f"sWC{b}"
            W("sync", s2n, val(s2n))
            emit("sync", fn(b))
            return b, ((s1n, inc(s1n, 16)), (s2n, inc(s2n, 16)))
        emit("sync", fn(b))
        return b, ((s1n, inc(s1n, 16)),)

    def ph_ffn(tile, ln_cnt):
        silu_cnt = {}
        up_ev = {}
        first = [True]
        last_xl_grp = [0]
        for phase, (wall, ccx) in enumerate(((wg_all, cc_g), (wu_all, cc_u))):
            for bi, (c0, w) in enumerate(FFN_BLOCKS):
                def mk(b, wall=wall, c0=c0, w=w):
                    def f(e):
                        e.dma_start(
                            out=wbuf[b][:, 0:16 * w].rearrange("p (kc o) -> p kc o", kc=16),
                            in_=wall.rearrange("(kc p) o -> p kc o", p=128)[:, :, c0:c0 + w],
                        ).then_inc(sems[f"sWB{b}"], 16)
                    return f
                b, ld = wload(mk, geo="A")
                for ocl in range(w // 128):
                    oc = 4 * bi + ocl
                    bank = 2 * phase + (oc % 2)
                    if ocl == 0:
                        for sname, scnt in ld:
                            W("tensor", sname, scnt)
                    if first[0]:
                        W("tensor", "sCC", ccx)
                        W("tensor", "sDVE", ln_cnt)
                        first[0] = False
                    if phase == 1 and oc < 2:
                        W("tensor", "sCC", cc_u)
                        use("tensor", f"PB{bank}")
                    if phase == 0 and oc < 2:
                        use("tensor", f"PB{bank}")
                    if phase == 0 and oc >= 2:
                        W("tensor", "sACT", silu_cnt[oc - 2])
                    if phase == 1 and oc >= 2:
                        W("tensor", "sDVE", up_ev[oc - 2])
                    for kc in range(16):
                        last = kc == 15
                        emit("tensor", lambda e, b=b, w=w, kc=kc, ocl=ocl, bank=bank, last=last: (
                            e.matmul(PB[bank][:, :],
                                     wbuf[b][:, w * kc + 128 * ocl: w * kc + 128 * (ocl + 1)],
                                     xl[:, TT * kc: TT * (kc + 1)], start=(kc == 0), stop=last,
                                     skip_group_check=True).then_inc(sems["sPE"], 1) if last else
                            e.matmul(PB[bank][:, :],
                                     wbuf[b][:, w * kc + 128 * ocl: w * kc + 128 * (ocl + 1)],
                                     xl[:, TT * kc: TT * (kc + 1)], start=(kc == 0), stop=last,
                                     skip_group_check=True)))
                    grp = inc("sPE")
                    if phase == 0:
                        W("scalar", "sPE", grp)
                        emit("scalar", lambda e, oc=oc, bank=bank: e.activation(
                            out=h_act[:, 512 * oc: 512 * (oc + 1)], in_=PB[bank][:, :],
                            func=AF.Silu).then_inc(sems["sACT"], 1))
                        silu_cnt[oc] = inc("sACT")
                    else:
                        W("vector", "sPE", grp)
                        W("vector", "sACT", silu_cnt[oc])
                        emit("vector", lambda e, oc=oc, bank=bank: e.tensor_tensor(
                            out=h_act[:, 512 * oc: 512 * (oc + 1)], in0=PB[bank][:, :],
                            in1=h_act[:, 512 * oc: 512 * (oc + 1)],
                            op=ALU.mult).then_inc(sems["sDVE"], 1))
                        up_ev[oc] = inc("sDVE")
                wring["consumed"][b] = val("sPE")
            last_xl_grp[0] = val("sPE")
        release("xl", "sPE", last_xl_grp[0])
        release("PB0", "sACT", silu_cnt[IC - 1])
        release("PB1", "sACT", silu_cnt[IC - 1])
        release("PB2", "sDVE", up_ev[IC - 1])
        release("PB3", "sDVE", up_ev[IC - 1])
        hact_done = up_ev[IC - 1]
        down_ev = {}
        for dc in range(DC):
            def mkd(b, dc=dc):
                def f(e):
                    e.dma_start(
                        out=wbuf[b][:, 0:II].rearrange("p (ic d) -> p ic d", ic=IC),
                        in_=wd_all.rearrange("(ic p) d -> p ic d", p=128)[:, :, 128 * dc: 128 * (dc + 1)],
                    ).then_inc(sems[f"sWB{b}"], 16)
                return f
            b, ld = wload(mkd, geo="A")
            bank = 4 + (dc % 2)
            for sname, scnt in ld:
                W("tensor", sname, scnt)
            if dc == 0:
                W("tensor", "sCC", cc_d)
                W("tensor", "sDVE", hact_done)
            if dc < 2:
                use("tensor", f"PB{bank}")
            if dc >= 2:
                W("tensor", "sDVE", down_ev[dc - 2])
            for ic in range(IC):
                last = ic == IC - 1
                emit("tensor", lambda e, b=b, ic=ic, bank=bank, last=last: (
                    e.matmul(PB[bank][:, :], wbuf[b][:, 128 * ic: 128 * (ic + 1)],
                             h_act[:, 512 * ic: 512 * (ic + 1)], start=(ic == 0), stop=last,
                             skip_group_check=True).then_inc(sems["sPE"], 1) if last else
                    e.matmul(PB[bank][:, :], wbuf[b][:, 128 * ic: 128 * (ic + 1)],
                             h_act[:, 512 * ic: 512 * (ic + 1)], start=(ic == 0), stop=last,
                             skip_group_check=True)))
            grp = inc("sPE")
            wring["consumed"][b] = grp
            W("vector", "sPE", grp)
            emit("vector", lambda e, dc=dc, tile=tile, bank=bank: e.scalar_tensor_tensor(
                out=xs[:, T * dc + TT * tile: T * dc + TT * (tile + 1)], in0=PB[bank][:, :],
                scalar=0.5, in1=xs[:, T * dc + TT * tile: T * dc + TT * (tile + 1)],
                op0=ALU.mult, op1=ALU.add).then_inc(sems["sDVE"], 1))
            down_ev[dc] = inc("sDVE")
        release("PB4", "sDVE", down_ev[DC - 1])
        release("PB5", "sDVE", down_ev[DC - 2])
        return val("sDVE")

    def dense_block_load256(m, db):
        def mk(b):
            def f(e):
                for par in range(2):
                    src_ = den_all.rearrange("(r m par q) d -> m par q r d",
                                             m=4, par=2, q=128)[m, par][:, :, 256 * db: 256 * (db + 1)]
                    dst = wbufq[b][:, 0:4096].rearrange("p (r par o) -> p par r o",
                                                        r=8, par=2)[:, par]
                    e.dma_start(out=dst, in_=src_).then_inc(
                        sems[f"sWB{b}" if par == 0 else f"sWC{b}"], 16)
            return f
        return wload(mk, two=True, geo="B")

    def qtile(b, kc, ocl):
        return wbufq[b][:, 512 * (kc // 2) + 256 * (kc % 2) + 128 * ocl:
                        512 * (kc // 2) + 256 * (kc % 2) + 128 * (ocl + 1)]

    def ph_qkv(ln_cnt):
        ks_dma = [0, 0]
        vs_dma = [0, 0]
        first = [True]
        xls = (xl, xl_t1)
        for m in (0, 1):
            ev = {}
            for db in range(8):
                b, ld = dense_block_load256(m, db)
                for ocl in range(2):
                    oc = 2 * db + ocl
                    banks = (oc % 2, 2 + oc % 2)
                    if ocl == 0:
                        for sname, scnt in ld:
                            W("tensor", sname, scnt)
                    if first[0]:
                        W("tensor", "sCC", cc_den)
                        W("tensor", "sDVE", ln_cnt)
                        first[0] = False
                    if oc < 2:
                        use("tensor", f"PB{banks[0]}")
                        use("tensor", f"PB{banks[1]}")
                    if oc >= 2:
                        W("tensor", "sACT", ev[(oc - 2, 1)])
                    for kc in range(16):
                        last = kc == 15
                        emit("tensor", lambda e, b=b, kc=kc, ocl=ocl, bk=banks[0]: e.matmul(
                            PB[bk], qtile(b, kc, ocl), xl[:, TT * kc: TT * (kc + 1)],
                            start=(kc == 0), stop=(kc == 15), skip_group_check=True))
                        emit("tensor", lambda e, b=b, kc=kc, ocl=ocl, bk=banks[1], last=last: (
                            e.matmul(PB[bk], qtile(b, kc, ocl), xl_t1[:, TT * kc: TT * (kc + 1)],
                                     start=(kc == 0), stop=(kc == 15),
                                     skip_group_check=True).then_inc(sems["sPE"], 1) if last else
                            e.matmul(PB[bk], qtile(b, kc, ocl), xl_t1[:, TT * kc: TT * (kc + 1)],
                                     start=(kc == 0), stop=(kc == 15), skip_group_check=True)))
                    grp = inc("sPE")
                    for tile in range(2):
                        bk = banks[tile]
                        W("scalar", "sPE", grp)
                        if m == 0:
                            emit("scalar", lambda e, oc=oc, tile=tile, bk=bk: e.activation(
                                out=Qb[:, T * oc + TT * tile: T * oc + TT * (tile + 1)],
                                in_=PB[bk], func=AF.Identity, scale=RS,
                                bias=bcol[:, oc: oc + 1]).then_inc(sems["sACT"], 1))
                            ev[(oc, tile)] = inc("sACT")
                        else:
                            ks = tile
                            csem = f"sCN{2 + ks}"
                            if ks_dma[ks]:
                                W("scalar", csem, ks_dma[ks])
                            emit("scalar", lambda e, oc=oc, ks=ks, bk=bk: e.activation(
                                out=kstage[ks][:, :], in_=PB[bk], func=AF.Identity,
                                bias=bcol[:, 16 + oc: 17 + oc]).then_inc(sems["sACT"], 1))
                            ev[(oc, tile)] = inc("sACT")
                            W("gpsimd", "sACT", ev[(oc, tile)])
                            ks_dma[ks] = qdma("gpsimd", csem,
                                              lambda e, oc=oc, ks=ks, tile=tile, csem=csem: e.dma_start(
                                out=kv_part[128 * oc: 128 * (oc + 1), TT * tile: TT * (tile + 1)],
                                in_=kstage[ks][:, :]).then_inc(sems[csem], 16))
                wring["consumed"][b] = val("sPE")
            release("PB0", "sACT", ev[(14, 0)])
            release("PB1", "sACT", ev[(15, 0)])
            release("PB2", "sACT", ev[(14, 1)])
            release("PB3", "sACT", ev[(15, 1)])
        cc_k = emit_cc(kv_part[0:2048, :], k_all, g4)
        v_view = kv_part[2048:4096, :].rearrange("(t two) c -> t (two c)", two=2)
        nv = [0]
        ev = {}
        for db in range(8):
            b, ld = dense_block_load256(2, db)
            for tile in range(2):
                for tcl in range(4):
                    bank = nv[0] % 2
                    if tile == 0 and tcl == 0:
                        for sname, scnt in ld:
                            W("tensor", sname, scnt)
                    if nv[0] < 2:
                        use("tensor", f"PB{bank}")
                        W("tensor", "sG", misc_g)
                    if nv[0] >= 2:
                        W("tensor", "sACT", ev[nv[0] - 2])
                    for kc in range(16):
                        emit("tensor", lambda e, b=b, kc=kc, tcl=tcl, tile=tile, bank=bank: e.matmul(
                            PB[bank][:, 0:256],
                            xls[tile][:, TT * kc + 128 * tcl: TT * kc + 128 * (tcl + 1)],
                            wbufq[b][:, 512 * (kc // 2) + 256 * (kc % 2):
                                     512 * (kc // 2) + 256 * (kc % 2) + 256],
                            start=(kc == 0), stop=False, skip_group_check=True))
                    emit("tensor", lambda e, db=db, bank=bank: e.matmul(
                        PB[bank][:, 0:256], onesb[0:1, 0:128], brow[0:1, 256 * db: 256 * (db + 1)],
                        start=False, stop=True, skip_group_check=True).then_inc(sems["sPE"], 1))
                    grp = inc("sPE")
                    W("scalar", "sPE", grp)
                    vs = nv[0] % 2
                    csem = f"sCN{2 + vs}"
                    if vs_dma[vs]:
                        W("scalar", csem, vs_dma[vs])
                    emit("scalar", lambda e, vs=vs, bank=bank: e.copy(
                        vstage[vs][:, 0:256], PB[bank][:, 0:256]).then_inc(sems["sACT"], 1))
                    ev[nv[0]] = inc("sACT")
                    W("gpsimd", "sACT", ev[nv[0]])
                    vs_dma[vs] = qdma("gpsimd", csem,
                                      lambda e, tile=tile, tcl=tcl, db=db, vs=vs, csem=csem: e.dma_start(
                        out=v_view[128 * (4 * tile + tcl): 128 * (4 * tile + tcl + 1),
                                   256 * db: 256 * (db + 1)],
                        in_=vstage[vs][:, 0:256]).then_inc(sems[csem], 16))
                    nv[0] += 1
            wring["consumed"][b] = val("sPE")
        release("xl", "sPE", val("sPE"))
        release("PB0", "sACT", ev[nv[0] - 2])
        release("PB1", "sACT", ev[nv[0] - 1])
        cc_v = emit_cc(kv_part[2048:4096, :], v_all, g4)
        return cc_k, cc_v

    def ph_attn(cc_k, cc_v):
        kld = [0, 0]
        kfree = [0, 0]
        vfree = [0, 0, 0]
        first_k = [True]
        at_free = [0]
        post_act = [0]
        AT = [P2[2], P2[3]]
        ROW = [(ti, bp, ch) for ti in range(4) for bp in (0, 32, 64) for ch in range(2)]

        for pair in range(8):
            h0, h1 = 2 * pair, 2 * pair + 1
            heads = ((0, h0), (1, h1))
            for hi, h in heads:
                if first_k[0] and hi == 0:
                    W("sync", "sCC", cc_k)
                    W("sync", "sPE", val("sPE"))
                    first_k[0] = False
                if kfree[hi]:
                    W("sync", "sPE", kfree[hi])
                ksem = f"sKH{hi}"
                kld[hi] = qdma("sync", ksem, lambda e, h=h, hi=hi, ksem=ksem: e.dma_start(
                    out=Kh[hi].rearrange("p (r c) -> p r c", r=4),
                    in_=k_all.rearrange("(r q) c -> q r c", r=4)[128 * h: 128 * (h + 1)]
                ).then_inc(sems[ksem], 16))

            sq_free = [0, 0]
            mm_cnt = {}
            first_row = [True]
            for hi, h in heads:
                for t in range(10):
                    sl = (10 * hi + t) % 2
                    if sq_free[sl]:
                        W("scalar", "sPE", sq_free[sl])
                    if t < 2:
                        emit("scalar", lambda e, h=h, t=t, sl=sl: e.activation(
                            out=sqs[sl][:, :], in_=Qb[:, T * h + TT * t: T * h + TT * (t + 1)],
                            func=AF.Square).then_inc(sems["sACT"], 1))
                    else:
                        if t == 2:
                            W("scalar", f"sKH{hi}", kld[hi])
                        emit("scalar", lambda e, hi=hi, t=t, sl=sl: e.activation(
                            out=sqs[sl][:, :], in_=Kh[hi][:, 512 * (t - 2): 512 * (t - 1)],
                            func=AF.Square).then_inc(sems["sACT"], 1))
                    sqc = inc("sACT")
                    rti, rbp, rch = ROW[10 * hi + t]
                    W("tensor", "sACT", sqc)
                    if first_row[0]:
                        if at_free[0]:
                            W("tensor", "sDVE", at_free[0])
                        if post_act[0]:
                            W("tensor", "sACT", post_act[0])
                        first_row[0] = False
                    emit("tensor", lambda e, rti=rti, rbp=rbp, rch=rch, sl=sl: e.matmul(
                        P2[rti][rbp: rbp + 1, 512 * rch: 512 * (rch + 1)],
                        onesb[:, 0:1], sqs[sl][:, :],
                        start=True, stop=True, skip_group_check=True).then_inc(sems["sPE"], 1))
                    mm_cnt[(hi, t)] = inc("sPE")
                    sq_free[sl] = mm_cnt[(hi, t)]
                W("vector", "sPE", mm_cnt[(hi, 9)])
                for t in range(10):
                    rti, rbp, rch = ROW[10 * hi + t]
                    emit("vector", lambda e, t=t, rti=rti, rbp=rbp, rch=rch: e.reduce_max(
                        crow[0:1, 4 + t: 5 + t],
                        P2[rti][rbp: rbp + 1, 512 * rch: 512 * (rch + 1)], axis=AX))
                emit("vector", lambda e: e.reduce_max(crow[0:1, 14:15], crow[0:1, 4:6], axis=AX))
                emit("vector", lambda e: e.reduce_max(crow[0:1, 15:16], crow[0:1, 6:14], axis=AX))
                emit("vector", lambda e: e.tensor_scalar_mul(
                    crow[0:1, 15:16], crow[0:1, 15:16], 1.0 / (2.0 * SQD)))
                emit("vector", lambda e: e.scalar_tensor_tensor(
                    out=crow[0:1, 14:15], in0=crow[0:1, 14:15], scalar=SQD / 2.0,
                    in1=crow[0:1, 15:16], op0=ALU.mult, op1=ALU.add))
                emit("vector", lambda e: e.tensor_scalar_mul(
                    crow[0:1, 14:15], crow[0:1, 14:15], -1.0).then_inc(sems["sDVE"], 1))
                cneg = inc("sDVE")
                W("tensor", "sDVE", cneg)
                emit("tensor", lambda e: e.matmul(
                    P2[0][:, 0:1], onesf[0:1, :], crow[0:1, 14:15],
                    start=True, stop=True, skip_group_check=True).then_inc(sems["sPE"], 1))
                bcc = inc("sPE")
                W("scalar", "sPE", bcc)
                emit("scalar", lambda e, hi=hi: e.copy(
                    crow[:, hi: hi + 1], P2[0][:, 0:1]).then_inc(sems["sACT"], 1))
                inc("sACT")
            cb_done = val("sACT")
            emit("vector", lambda e: e.memset(sacc4[0][:, :], 0.0))
            emit("vector", lambda e: e.memset(sacc4[1][:, :], 0.0).then_inc(sems["sDVE"], 1))
            sacc_d = inc("sDVE")
            if at_free[0]:
                W("gpsimd", "sDVE", at_free[0])
            emit("gpsimd", lambda e: e.memset(sacc4[2][:, :], 0.0))
            emit("gpsimd", lambda e: e.memset(sacc4[3][:, :], 0.0).then_inc(sems["sPOOL"], 1))
            sacc_p = inc("sPOOL")

            exp_cnt = [0, 0]
            av_pe = [0, 0]
            b1c = [0, 0]
            vldc = {}
            for j in range(33):
                if j < 32:
                    vb = j % 3
                    vsem = f"sV{vb}"
                    if vfree[vb]:
                        W("sync", "sPE", vfree[vb])
                    if j == 0 and pair == 0:
                        W("sync", "sCC", cc_v)
                    vld = qdma("sync", vsem, lambda e, j=j, vsem=vsem: e.dma_start(
                        out=Vb[j % 3][:, :],
                        in_=v_all[2048 * (j // 8) + 256 * (j % 8):
                                  2048 * (j // 8) + 256 * (j % 8 + 1), :]
                        .rearrange("(p two) c -> p (two c)", two=2)).then_inc(sems[vsem], 16))
                    vldc[j] = (vsem, vld)
                    for hh, h in heads:
                        if j == 0:
                            W("tensor", "sACT", cb_done)
                            W("tensor", f"sKH{hh}", kld[hh])
                        if exp_cnt[hh]:
                            W("tensor", "sACT", exp_cnt[hh])
                        emit("tensor", lambda e, hh=hh, h=h, j=j: e.matmul(
                            P2[hh][:, 0:512], Kh[hh][:, 128 * j: 128 * (j + 1)],
                            Qb[:, T * h: T * h + TT],
                            start=True, stop=True, skip_group_check=True))
                        emit("tensor", lambda e, hh=hh, h=h, j=j: e.matmul(
                            P2[hh][:, 512:1024], Kh[hh][:, 128 * j: 128 * (j + 1)],
                            Qb[:, T * h + TT: T * h + 2 * TT],
                            start=True, stop=True, skip_group_check=True).then_inc(sems["sPE"], 1))
                        b1c[hh] = inc("sPE")
                if j >= 1:
                    jj = j - 1
                    for hh, h in heads:
                        W("tensor", "sACT", exp_cnt[hh])
                        if hh == 0:
                            W("tensor", vldc[jj][0], vldc[jj][1])
                            if jj == 0 and at_free[0]:
                                W("tensor", "sDVE", at_free[0])
                        hcol = 128 * h
                        emit("tensor", lambda e, hh=hh, jj=jj, hcol=hcol: e.matmul(
                            AT[hh][:, 0:512], Vb[jj % 3][:, hcol: hcol + 128],
                            PTv[hh][:, 0:512],
                            start=(jj == 0), stop=(jj == 31), skip_group_check=True))
                        emit("tensor", lambda e, hh=hh, jj=jj, hcol=hcol: e.matmul(
                            AT[hh][:, 512:1024], Vb[jj % 3][:, hcol: hcol + 128],
                            PTv[hh][:, 512:1024],
                            start=(jj == 0), stop=(jj == 31),
                            skip_group_check=True).then_inc(sems["sPE"], 1))
                        av_pe[hh] = inc("sPE")
                    vfree[jj % 3] = av_pe[1]
                    W("vector", "sACT", exp_cnt[0])
                    emit("vector", lambda e: e.tensor_tensor(
                        out=sacc4[0][:, :], in0=sacc4[0][:, :], in1=PTv[0][:, 0:512],
                        op=ALU.add))
                    emit("vector", lambda e: e.tensor_tensor(
                        out=sacc4[1][:, :], in0=sacc4[1][:, :], in1=PTv[0][:, 512:1024],
                        op=ALU.add).then_inc(sems["sDVE"], 1))
                    sacc_d = inc("sDVE")
                    W("gpsimd", "sACT", exp_cnt[1])
                    emit("gpsimd", lambda e: e.tensor_tensor(
                        out=sacc4[2][:, :], in0=sacc4[2][:, :], in1=PTv[1][:, 0:512],
                        op=ALU.add))
                    emit("gpsimd", lambda e: e.tensor_tensor(
                        out=sacc4[3][:, :], in0=sacc4[3][:, :], in1=PTv[1][:, 512:1024],
                        op=ALU.add).then_inc(sems["sPOOL"], 1))
                    sacc_p = inc("sPOOL")
                if j < 32:
                    for hh, h in heads:
                        W("scalar", "sPE", b1c[hh])
                        if j > 0:
                            W("scalar", "sPE", av_pe[hh])
                            if hh == 0:
                                W("scalar", "sDVE", sacc_d)
                            else:
                                W("scalar", "sPOOL", sacc_p)
                        else:
                            if hh == 0:
                                W("scalar", "sDVE", sacc_d)
                            else:
                                W("scalar", "sPOOL", sacc_p)
                        emit("scalar", lambda e, hh=hh: e.activation(
                            out=PTv[hh][:, :], in_=P2[hh][:, :], func=AF.Exp,
                            bias=crow[:, hh: hh + 1]).then_inc(sems["sACT"], 1))
                        exp_cnt[hh] = inc("sACT")
            kfree[0] = b1c[0]
            kfree[1] = b1c[1]

            W("tensor", "sDVE", sacc_d)
            W("tensor", "sPOOL", sacc_p)
            W("tensor", "sACT", max(exp_cnt))
            rrows = [(0, 0, 0), (0, 32, 0), (0, 64, 0), (0, 0, 1)]
            for i in range(4):
                rti, rbp, rch = rrows[i]
                emit("tensor", lambda e, i=i, rti=rti, rbp=rbp, rch=rch: e.matmul(
                    P2[rti][rbp: rbp + 1, 512 * rch: 512 * (rch + 1)],
                    onesf[:, 0:1], sacc4[i][:, :],
                    start=True, stop=True,
                    skip_group_check=True).then_inc(sems["sPE"], 1) if i == 3 else e.matmul(
                    P2[rti][rbp: rbp + 1, 512 * rch: 512 * (rch + 1)],
                    onesf[:, 0:1], sacc4[i][:, :],
                    start=True, stop=True, skip_group_check=True))
            srow = inc("sPE")
            W("vector", "sPE", srow)
            emit("vector", lambda e: e.reciprocal(rowp2[0:1, :], P2[0][0:1, 0:512]))
            emit("vector", lambda e: e.reciprocal(rowp2[32:33, :], P2[0][32:33, 0:512]))
            emit("vector", lambda e: e.reciprocal(rowp2[64:65, :], P2[0][64:65, 0:512]))
            emit("vector", lambda e: e.reciprocal(
                rowp[0:1, :], P2[0][0:1, 512:1024]).then_inc(sems["sDVE"], 1))
            rc = inc("sDVE")
            rsrcs = [(rowp2[0:1, :], 0), (rowp2[32:33, :], 32),
                     (rowp2[64:65, :], 64), (rowp[0:1, :], 0)]
            ev_dve = {}
            bc_ev = {}
            for s in range(4):
                hh, sh = s // 2, s % 2
                rsrc, rb = rsrcs[s]
                bankv = P2[1][:, 0:512] if s % 2 == 0 else P2[1][:, 512:1024]
                bdst = Abc if s % 2 == 0 else Bbc
                W("tensor", "sDVE", rc)
                if s < 2:
                    W("tensor", "sACT", exp_cnt[1])
                else:
                    W("tensor", "sDVE", ev_dve[s - 2])
                emit("tensor", lambda e, rsrc=rsrc, rb=rb, bankv=bankv: e.matmul(
                    bankv, onesf[rb: rb + 1, :], rsrc, start=True, stop=True,
                    skip_group_check=True).then_inc(sems["sPE"], 1))
                bcx = inc("sPE")
                W("scalar", "sPE", bcx)
                emit("scalar", lambda e, bdst=bdst, bankv=bankv: e.copy(
                    bdst[:, :], bankv).then_inc(sems["sACT"], 1))
                bc_ev[s] = inc("sACT")
                W("vector", "sACT", bc_ev[s])
                W("vector", "sPE", av_pe[hh])
                h = (h0, h1)[hh]
                emit("vector", lambda e, hh=hh, sh=sh, h=h, bdst=bdst: e.tensor_tensor(
                    out=attn_s[:, T * h + TT * sh: T * h + TT * (sh + 1)],
                    in0=AT[hh][:, 512 * sh: 512 * (sh + 1)], in1=bdst[:, :],
                    op=ALU.mult).then_inc(sems["sDVE"], 1))
                ev_dve[s] = inc("sDVE")
            at_free[0] = val("sDVE")
            post_act[0] = val("sACT")
        release("PR", "sDVE", at_free[0])
        for i in range(6):
            release(f"PB{i}", "sDVE", at_free[0])
        return at_free[0]

    def ph_oproj(attn_done):
        oev = {}
        no = [0]
        W("sync", "sPE", val("sPE"))
        for db in range(8):
            b, ld = dense_block_load256(3, db)
            for dcl in range(2):
                dc = 2 * db + dcl
                banks = (no[0] % 2, 2 + no[0] % 2)
                if dcl == 0:
                    for sname, scnt in ld:
                        W("tensor", sname, scnt)
                if no[0] == 0:
                    W("tensor", "sDVE", attn_done)
                    W("tensor", "sG", misc_g)
                if no[0] < 2:
                    use("tensor", f"PB{banks[0]}")
                    use("tensor", f"PB{banks[1]}")
                if no[0] >= 2:
                    W("tensor", "sDVE", oev[(no[0] - 2, 1)])
                for oc in range(16):
                    emit("tensor", lambda e, b=b, oc=oc, dcl=dcl, bk=banks[0]: e.matmul(
                        PB[bk], qtile(b, oc, dcl),
                        attn_s[:, T * oc: T * oc + TT],
                        start=(oc == 0), stop=False, skip_group_check=True))
                    emit("tensor", lambda e, b=b, oc=oc, dcl=dcl, bk=banks[1]: e.matmul(
                        PB[bk], qtile(b, oc, dcl),
                        attn_s[:, T * oc + TT: T * oc + 2 * TT],
                        start=(oc == 0), stop=False, skip_group_check=True))
                emit("tensor", lambda e, dc=dc, bk=banks[0]: e.matmul(
                    PB[bk], brow[32:33, 128 * dc: 128 * (dc + 1)], onesb[32:33, :],
                    start=False, stop=True, skip_group_check=True))
                emit("tensor", lambda e, dc=dc, bk=banks[1]: e.matmul(
                    PB[bk], brow[32:33, 128 * dc: 128 * (dc + 1)], onesb[32:33, :],
                    start=False, stop=True, skip_group_check=True).then_inc(sems["sPE"], 1))
                grp = inc("sPE")
                for th in range(2):
                    bk = banks[th]
                    W("vector", "sPE", grp)
                    emit("vector", lambda e, dc=dc, th=th, bk=bk: e.tensor_tensor(
                        out=xs[:, T * dc + TT * th: T * dc + TT * (th + 1)], in0=PB[bk],
                        in1=xs[:, T * dc + TT * th: T * dc + TT * (th + 1)],
                        op=ALU.add).then_inc(sems["sDVE"], 1))
                    oev[(no[0], th)] = inc("sDVE")
                no[0] += 1
            wring["consumed"][b] = val("sPE")
        release("PB0", "sDVE", oev[(no[0] - 2, 0)])
        release("PB1", "sDVE", oev[(no[0] - 1, 0)])
        release("PB2", "sDVE", oev[(no[0] - 2, 1)])
        release("PB3", "sDVE", oev[(no[0] - 1, 1)])
        return val("sDVE")

    def ph_xout(prev_dve):
        W("tensor", "sDVE", prev_dve)
        use("tensor", "PB0")
        use("tensor", "PB1")
        use("tensor", "PB2")
        slot_free2 = [("sDVE", 0)] * NFS
        st_done = [0, 0]
        for j in range(8):
            if st_done[j % 2]:
                W("vector", f"sST{j % 2}", st_done[j % 2])
            for k in range(16):
                sl = k % NFS
                W("tensor", slot_free2[sl][0], slot_free2[sl][1])
                emit("tensor", lambda e, j=j, k=k, sl=sl: e.transpose(
                    fslot(sl), xs[:, T * k + 128 * j: T * k + 128 * (j + 1)],
                    identf[:]).then_inc(sems["sPE"], 1))
                tpc = inc("sPE")
                W("vector", "sPE", tpc)
                emit("vector", lambda e, j=j, k=k, sl=sl: e.tensor_copy(
                    ostage[j % 2][:, 128 * k: 128 * (k + 1)], fslot(sl)).then_inc(sems["sDVE"], 1))
                slot_free2[sl] = ("sDVE", inc("sDVE"))
            W("gpsimd", "sDVE", val("sDVE"))
            ssem = f"sST{j % 2}"
            st_done[j % 2] = qdma("gpsimd", ssem, lambda e, j=j, ssem=ssem: e.dma_start(
                out=dy[128 * j: 128 * (j + 1), :], in_=ostage[j % 2]).then_inc(sems[ssem], 16))
        W("gpsimd", "sST0", val("sST0"))
        W("gpsimd", "sST1", val("sST1"))

    if STAGE >= 1:
        for tile in range(2):
            lc = ph_ln(tile)
            ph_ffn(tile, lc)
    if STAGE >= 2:
        ph_ln(0)
        lc = ph_ln(1, xl_t1)
        cc_k, cc_v = ph_qkv(lc)
        adone = ph_attn(cc_k, cc_v)
        prev = ph_oproj(adone)
    else:
        prev = val("sDVE")
    if STAGE >= 3:
        for tile in range(2):
            lc = ph_ln(tile)
            ph_ffn(tile, lc)
        prev = val("sDVE")
    ph_xout(prev)

    with nc.Block() as block:
        @block.sync
        def _(e):
            for f in ops["sync"]:
                f(e)

        @block.tensor
        def _(e):
            for f in ops["tensor"]:
                f(e)

        @block.vector
        def _(e):
            for f in ops["vector"]:
                f(e)

        @block.scalar
        def _(e):
            for f in ops["scalar"]:
                f(e)

        @block.gpsimd
        def _(e):
            for f in ops["gpsimd"]:
                f(e)

    es.close()
    return nc


_CACHE = {}


def kernel(**inputs):
    x = np.asarray(inputs["L0_attn_hidden_states"], dtype=np.float32).reshape(8192, D)
    lut = np.ascontiguousarray(np.asarray(inputs["L0_ffn_lut"], dtype=np.float32))
    gidx = np.asarray(inputs["L0_ffn_gate_idx"], dtype=np.int32)
    uidx = np.asarray(inputs["L0_ffn_up_idx"], dtype=np.int32)
    didx = np.asarray(inputs["L0_ffn_down_idx"], dtype=np.int32)
    dense = {nm: np.asarray(inputs[f"L0_attn_{nm}"], dtype=np.float32)
             for nm in ("wq", "wk", "wv", "wo")}
    bias = {nm: np.ascontiguousarray(np.asarray(inputs[f"L0_attn_{nm}"], dtype=np.float32))
            for nm in ("bq", "bk", "bv", "bo")}

    if "nc" not in _CACHE:
        _CACHE["nc"] = build_nc()
    nc = _CACHE["nc"]

    in_maps = []
    for c in range(NCORES):
        m = {
            "x": np.ascontiguousarray(x[T * c: T * (c + 1)]),
            "gidx": np.ascontiguousarray(gidx[:, 256 * c: 256 * (c + 1)]),
            "uidx": np.ascontiguousarray(uidx[:, 256 * c: 256 * (c + 1)]),
            "didx": np.ascontiguousarray(didx[:, 688 * c: 688 * (c + 1)]),
            "lut": lut,
        }
        for nm in ("wq", "wk", "wv", "wo"):
            m[nm] = np.ascontiguousarray(dense[nm][:, 256 * c: 256 * (c + 1)])
        for nm in ("bq", "bk", "bv", "bo"):
            m[nm] = bias[nm]
        in_maps.append(m)

    res = run_bass_kernel_spmd(nc, in_maps, core_ids=list(range(NCORES)))
    _CACHE["last_res"] = res
    y = np.concatenate([res.results[c]["y"] for c in range(NCORES)], axis=0)
    return y.reshape(2, 4096, D).astype(np.float32)
